# version 23
# baseline (speedup 1.0000x reference)
import numpy as np
import ml_dtypes
from contextlib import ExitStack

import concourse.bass as bass
import concourse.mybir as mybir
from concourse.bass_utils import run_bass_kernel_spmd

F32 = mybir.dt.float32
BF16 = mybir.dt.bfloat16
AF = mybir.ActivationFunctionType
ALU = mybir.AluOpType
AX = mybir.AxisListType
bf16 = ml_dtypes.bfloat16

T, D, E3 = 2048, 1024, 3072
NT, ND, NB, NH = 16, 8, 4, 16
INV_D = 1.0 / 1024.0
LN_EPS = 1e-5


class _Emitter:

    ENGINES = ("sync", "tensor", "scalar", "vector", "gpsimd")
    DMAQ = {"sync": "dq_s", "gpsimd": "dq_g", "scalar": "dq_a"}
    RING = 12

    def __init__(self):
        self.ops = {e: [] for e in self.ENGINES}
        self.count = {}
        self.water = {}
        self.regions = {}
        self.dma_idx = {q: 0 for q in self.DMAQ.values()}
        self.log = []
        for e in self.ENGINES:
            self.count[e] = 0
        for q in self.DMAQ.values():
            for k in range(self.RING):
                self.count[f"{q}_{k}"] = 0

    def sem_names(self):
        names = list(self.ENGINES)
        for q in self.DMAQ.values():
            names += [f"{q}_{k}" for k in range(self.RING)]
        return names

    def _deps_for(self, reads, writes):
        deps = []
        for key, lo, hi in reads:
            for seg in self.regions.get(key, ()):
                if seg[0] < hi and lo < seg[1] and seg[2] is not None:
                    deps.append(seg[2])
        for key, lo, hi in writes:
            for seg in self.regions.get(key, ()):
                if seg[0] < hi and lo < seg[1]:
                    if seg[2] is not None:
                        deps.append(seg[2])
                    deps.extend(seg[3].values())
        return deps

    def _record(self, reads, writes, tok):
        for key, lo, hi in reads:
            segs = self.regions.setdefault(key, [])
            hit = False
            for seg in segs:
                if seg[0] == lo and seg[1] == hi:
                    seg[3][tok[0]] = tok
                    hit = True
            if not hit:
                segs.append([lo, hi, None, {tok[0]: tok}])
        for key, lo, hi in writes:
            segs = self.regions.setdefault(key, [])
            keep = []
            for seg in segs:
                if seg[0] < hi and lo < seg[1]:
                    if seg[0] < lo:
                        keep.append([seg[0], lo, seg[2], dict(seg[3])])
                    if hi < seg[1]:
                        keep.append([hi, seg[1], seg[2], dict(seg[3])])
                else:
                    keep.append(seg)
            keep.append([lo, hi, tok, {}])
            self.regions[key] = keep

    def _emit_waits(self, eng, deps):
        waits = []
        best = {}
        for sem, val in deps:
            if val > best.get(sem, 0):
                best[sem] = val
        for sem, val in best.items():
            if self.water.get((eng, sem), 0) >= val:
                continue
            self.water[(eng, sem)] = val
            waits.append((sem, val))
        return waits

    def op(self, eng, fns, r=(), w=(), dma=False, extra_deps=()):
        if not isinstance(fns, (list, tuple)):
            fns = [fns]
        deps = self._deps_for(r, w)
        deps.extend(extra_deps)
        if dma:
            q = self.DMAQ[eng]
            sem = f"{q}_{self.dma_idx[q] % self.RING}"
            self.dma_idx[q] += 1
            if self.count[sem] > 0:
                deps.append((sem, self.count[sem]))
            waits = self._emit_waits(eng, deps)
            self.count[sem] += 16
            tok = (sem, self.count[sem])
            inc = (sem, 16)
        else:
            waits = self._emit_waits(eng, deps)
        if not dma:
            self.count[eng] += 1
            tok = (eng, self.count[eng])
            inc = (eng, 1)
        self.ops[eng].append((waits, fns, inc))
        self.log.append((eng, tuple(r), tuple(w), tok))
        self._record(r, w, tok)
        return tok

    def barrier(self, eng, toks):
        waits = self._emit_waits(eng, list(toks))
        if waits:
            self.ops[eng].append((waits, [], None))

    def replay(self, nc, handles, sems):
        for eng_name in self.ENGINES:
            oplist = self.ops[eng_name]
            h = handles[eng_name]
            for waits, fns, inc in oplist:
                for sem, val in waits:
                    h.wait_ge(sems[sem], val)
                inst = None
                for fn in fns:
                    inst = fn(h)
                if inc is not None and inst is not None:
                    inst.then_inc(sems[inc[0]], inc[1])


def _host_prep(inp):
    f = np.float32
    Wqkv = np.asarray(inp["L0_attn_W_qkv"], f)
    bqkv = np.asarray(inp["L0_attn_b_qkv"], f)
    d = {}
    d["wqkvt"] = np.ascontiguousarray(
        Wqkv.reshape(24, 128, 8, 128).transpose(0, 3, 2, 1)[:16]).astype(bf16)
    d["wv_nat"] = np.ascontiguousarray(Wqkv[2048:3072].T).astype(bf16)
    d["bqkv_cols"] = np.ascontiguousarray(bqkv.reshape(24, 128).T)
    d["bv_row"] = np.ascontiguousarray(bqkv[2048:].reshape(1, 1024))
    for nm, key in (("wq_e", "L0_ffn_Wq"), ("wk_e", "L0_ffn_Wk"),
                    ("wv_e", "L0_ffn_Wv")):
        d[nm] = np.ascontiguousarray(np.asarray(inp[key], f))
    bexp = []
    for key in ("L0_ffn_bq", "L0_ffn_bk", "L0_ffn_bv"):
        b = np.asarray(inp[key], f)
        bexp.append(b.reshape(4, 8, 128).transpose(2, 1, 0).reshape(128, 32))
    d["bexpT"] = np.ascontiguousarray(np.concatenate(bexp, axis=1))
    Wg1 = np.asarray(inp["L0_ffn_Wg1"], f)
    d["wg1t"] = np.ascontiguousarray(
        Wg1.reshape(8, 128, 128).transpose(1, 0, 2).reshape(128, 1024))
    d["wg2"] = np.ascontiguousarray(np.asarray(inp["L0_ffn_Wg2"], f))
    d["bg1_row"] = np.asarray(inp["L0_ffn_bg1"], f).reshape(1, 128)
    d["bg2_row"] = np.asarray(inp["L0_ffn_bg2"], f).reshape(1, 4)
    d["wo_b"] = np.ascontiguousarray(np.asarray(inp["L0_ffn_Wo"], f)).astype(bf16)
    d["bo_row"] = np.asarray(inp["L0_ffn_bo"], f).reshape(1, 1024)
    masks = np.zeros((128, 2048), dtype=bf16)
    for idx in range(4):
        p = np.arange(128)[:, None]
        n = np.arange(512)[None, :]
        masks[:, idx * 512:(idx + 1) * 512] = (p > n - 128 * idx).astype(bf16)
    d["masks"] = masks
    d["negid"] = (np.eye(128, dtype=np.float32) * -1e9).astype(bf16)
    d["onesf_row"] = np.ones((1, 128), f)
    d["onesf_col"] = np.ones((128, 1), f)
    d["id4"] = np.eye(4, dtype=f)
    return d


def build_nc():
    nc = bass.Bass("TRN2", target_bir_lowering=False, debug=False)

    x_h = nc.dram_tensor("x", [T, D], F32, kind="ExternalInput")
    wqkvt_h = nc.dram_tensor("wqkvt", [16, 128, 8, 128], BF16, kind="ExternalInput")
    wv_nat_h = nc.dram_tensor("wv_nat", [1024, 1024], BF16, kind="ExternalInput")
    bqkvc_h = nc.dram_tensor("bqkv_cols", [128, 24], F32, kind="ExternalInput")
    bvrow_h = nc.dram_tensor("bv_row", [1, 1024], F32, kind="ExternalInput")
    wq_e_h = nc.dram_tensor("wq_e", [4, 1024, 1024], F32, kind="ExternalInput")
    wk_e_h = nc.dram_tensor("wk_e", [4, 1024, 1024], F32, kind="ExternalInput")
    wv_e_h = nc.dram_tensor("wv_e", [4, 1024, 1024], F32, kind="ExternalInput")
    bexpT_h = nc.dram_tensor("bexpT", [128, 96], F32, kind="ExternalInput")
    wg1t_h = nc.dram_tensor("wg1t", [128, 1024], F32, kind="ExternalInput")
    wg2_h = nc.dram_tensor("wg2", [128, 4], F32, kind="ExternalInput")
    bg1_h = nc.dram_tensor("bg1_row", [1, 128], F32, kind="ExternalInput")
    bg2_h = nc.dram_tensor("bg2_row", [1, 4], F32, kind="ExternalInput")
    wo_h = nc.dram_tensor("wo_b", [1024, 1024], BF16, kind="ExternalInput")
    bo_h = nc.dram_tensor("bo_row", [1, 1024], F32, kind="ExternalInput")
    masks_h = nc.dram_tensor("masks", [128, 2048], BF16, kind="ExternalInput")
    negid_h = nc.dram_tensor("negid", [128, 128], BF16, kind="ExternalInput")
    nrm_h = nc.dram_tensor("nrm_scr", [16, 512], F32, kind="Internal")
    ones_r_h = nc.dram_tensor("onesf_row", [1, 128], F32, kind="ExternalInput")
    ones_c_h = nc.dram_tensor("onesf_col", [128, 1], F32, kind="ExternalInput")
    id4_h = nc.dram_tensor("id4", [4, 4], F32, kind="ExternalInput")

    xn_h = nc.dram_tensor("xn_scr", [T, D], BF16, kind="Internal")
    ya_h = nc.dram_tensor("ya_scr", [T, D], F32, kind="Internal")
    out_h = nc.dram_tensor("out", [T, D], F32, kind="ExternalOutput")

    E = _Emitter()
    global _LAST_E
    _LAST_E = E

    with ExitStack() as st:
        def sb(name, shape, dt):
            return st.enter_context(nc.sbuf_tensor(name, shape, dt))

        def ps(name, shape):
            return st.enter_context(nc.psum_tensor(name, shape, F32))

        big_a = sb("big_a", [128, 32768], BF16)
        big_b = sb("big_b", [128, 16640], BF16)
        big_c = sb("big_c", [128, 16384], BF16)
        big_d = sb("big_d", [128, 16384], BF16)
        wstr = sb("wstr", [128, 4096], F32)
        fbuf = sb("fbuf", [128, 3072], F32)
        masks = sb("masks_s", [128, 2048], BF16)
        negid = sb("negid_s", [128, 128], BF16)
        wg1 = sb("wg1_s", [128, 1024], F32)
        bexpT = sb("bexpT_s", [128, 96], F32)
        bqkvc = sb("bqkvc_s", [128, 24], F32)
        bvrow = sb("bvrow_s", [1, 1024], F32)
        borow = sb("borow_s", [1, 1024], F32)
        onesr = sb("onesr_s", [1, 128], F32)
        onesc = sb("onesc_s", [128, 1], F32)
        id4 = sb("id4_s", [4, 4], F32)
        wg2 = sb("wg2_s", [128, 4], F32)
        bg1 = sb("bg1_s", [1, 128], F32)
        bg2 = sb("bg2_s", [1, 4], F32)
        st_r = sb("st_r", [128, 16], F32)
        st_sum = sb("st_sum", [128, 16], F32)
        st_sq = sb("st_sq", [128, 16], F32)
        st_m = sb("st_m", [128, 16], F32)
        st_v = sb("st_v", [128, 16], F32)
        st_rm = sb("st_rm", [128, 16], F32)
        st_nmr = sb("st_nmr", [128, 16], F32)
        rc = sb("rc", [128, 4], F32)
        gsm = sb("gsm", [128, 40], F32)
        bc_cols = sb("bc_cols", [128, 24], F32)
        bcv_b = sb("bcv_b", [128, 8], BF16)
        grow = sb("grow", [4, 48], F32)
        hrow = sb("hrow", [1, 128], F32)
        lrow = sb("lrow", [1, 16], F32)

        pA = ps("pA", [128, 512])
        pB = ps("pB", [128, 512])
        pC = ps("pC", [128, 512])
        pD = ps("pD", [128, 512])
        pEF = ps("pEF", [128, 2048])
        pE = pEF[:, 0:1024]
        pF = pEF[:, 1024:2048]

        sems = {}
        for s in E.sem_names():
            sems[s] = st.enter_context(nc.semaphore(f"sem_{s}"))

        def bslice(tensor, c0, c1, p0=0, p1=128):
            return tensor[p0:p1, c0:c1]

        for name, hbm, sbuf_t, cols in (
            ("masks", masks_h, masks, 2048), ("negid", negid_h, negid, 128),
            ("wg1", wg1t_h, wg1, 1024),
            ("bexpT", bexpT_h, bexpT, 96), ("bqkvc", bqkvc_h, bqkvc, 24),
            ("wg2", wg2_h, wg2, 4), ("onesc", ones_c_h, onesc, 1),
        ):
            E.op("sync", (lambda h=hbm, s=sbuf_t: (lambda e: e.dma_start(out=s[:, :], in_=h[:, :])))(),
                 w=[(name, 0, cols)], dma=True)
        for name, hbm, sbuf_t in (
            ("bvrow", bvrow_h, bvrow), ("borow", bo_h, borow),
            ("onesr", ones_r_h, onesr), ("bg1", bg1_h, bg1), ("bg2", bg2_h, bg2),
        ):
            E.op("sync", (lambda h=hbm, s=sbuf_t: (lambda e: e.dma_start(out=s[:, :], in_=h[:, :])))(),
                 w=[(name, 0, 4096)], dma=True)
        E.op("sync", lambda e: e.dma_start(out=id4[:, :], in_=id4_h[:, :]),
             w=[("id4", 0, 4)], dma=True)

        XS = [0, 1024]
        for tt in range(NT):
            s = XS[tt % 2]
            xa = fbuf[:, s:s + 1024]
            E.op("scalar", (lambda tt=tt, xa=xa: (lambda e: e.dma_start(
                out=xa, in_=x_h[tt * 128:(tt + 1) * 128, :])))(),
                r=[], w=[("fbuf", s, s + 1024)], dma=True)
            E.op("scalar", (lambda xa=xa, tt=tt: (lambda e: e.activation(
                fbuf[:, 2048:3072], xa, AF.Square,
                accum_out=st_sq[:, tt:tt + 1])))(),
                r=[("fbuf", s, s + 1024)],
                w=[("fbuf", 2048, 3072), ("st_sq", tt, tt + 1)])
            E.op("vector", (lambda xa=xa, tt=tt: (lambda e: e.reduce_sum(
                st_sum[:, tt:tt + 1], xa, axis=AX.X)))(),
                r=[("fbuf", s, s + 1024)], w=[("st_sum", tt, tt + 1)])
        E.op("vector", lambda e: e.tensor_scalar_mul(st_m[:, :], st_sum[:, :], INV_D),
             r=[("st_sum", 0, 16)], w=[("st_m", 0, 16)])
        E.op("vector", lambda e: e.tensor_scalar_mul(st_v[:, :], st_sq[:, :], INV_D),
             r=[("st_sq", 0, 16)], w=[("st_v", 0, 16)])
        E.op("scalar", lambda e: e.activation(st_sq[:, :], st_m[:, :], AF.Square),
             r=[("st_m", 0, 16)], w=[("st_sq", 0, 16)])
        E.op("vector", lambda e: e.tensor_sub(st_v[:, :], st_v[:, :], st_sq[:, :]),
             r=[("st_v", 0, 16), ("st_sq", 0, 16)], w=[("st_v", 0, 16)])
        E.op("vector", lambda e: e.tensor_scalar_add(st_v[:, :], st_v[:, :], LN_EPS),
             r=[("st_v", 0, 16)], w=[("st_v", 0, 16)])
        E.op("scalar", lambda e: e.activation(st_sq[:, :], st_v[:, :], AF.Ln),
             r=[("st_v", 0, 16)], w=[("st_sq", 0, 16)])
        E.op("scalar", lambda e: e.activation(st_r[:, :], st_sq[:, :], AF.Exp, scale=-0.5),
             r=[("st_sq", 0, 16)], w=[("st_r", 0, 16)])
        E.op("vector", lambda e: e.tensor_mul(st_rm[:, :], st_r[:, :], st_m[:, :]),
             r=[("st_r", 0, 16), ("st_m", 0, 16)], w=[("st_rm", 0, 16)])
        E.op("vector", lambda e: e.scalar_tensor_tensor(
            st_nmr[:, :], st_m[:, :], -1.0, st_r[:, :], ALU.mult, ALU.mult),
            r=[("st_m", 0, 16), ("st_r", 0, 16)], w=[("st_nmr", 0, 16)])
        for tt in range(NT):
            s = XS[tt % 2]
            xa = fbuf[:, s:s + 1024]
            E.op("scalar", (lambda tt=tt, xa=xa: (lambda e: e.dma_start(
                out=xa, in_=x_h[tt * 128:(tt + 1) * 128, :])))(),
                r=[], w=[("fbuf", s, s + 1024)], dma=True)
            xns = 8192 + (tt % 2) * 1024
            xnb = big_c[:, xns:xns + 1024]
            E.op("vector", (lambda xa=xa, xnb=xnb, tt=tt: (lambda e: e.tensor_scalar(
                xnb, xa, st_r[:, tt:tt + 1], st_nmr[:, tt:tt + 1], ALU.mult, ALU.add)))(),
                r=[("fbuf", s, s + 1024), ("st_r", tt, tt + 1), ("st_nmr", tt, tt + 1)],
                w=[("big_c", xns, xns + 1024)])
            E.op("sync", (lambda xnb=xnb, tt=tt: (lambda e: e.dma_start(
                out=xn_h[tt * 128:(tt + 1) * 128, :], in_=xnb)))(),
                r=[("big_c", xns, xns + 1024)],
                w=[("xn_h", tt * 128, (tt + 1) * 128)], dma=True)
            first, last = tt == 0, tt == NT - 1
            E.op("tensor", [
                (lambda xa=xa, tt=tt, first=first, last=last: (lambda e: e.matmul(
                    pA[0:1, 0:512], st_r[:, tt:tt + 1], xa[:, 0:512],
                    start=first, stop=last)))(),
                (lambda xa=xa, tt=tt, first=first, last=last: (lambda e: e.matmul(
                    pB[0:1, 0:512], st_r[:, tt:tt + 1], xa[:, 512:1024],
                    start=first, stop=last)))(),
                (lambda tt=tt, first=first, last=last: (lambda e: e.matmul(
                    pC[0:1, 0:1], st_rm[:, tt:tt + 1], onesc[:, 0:1],
                    start=first, stop=last)))(),
            ], r=[("fbuf", s, s + 1024), ("st_r", tt, tt + 1),
                  ("st_rm", tt, tt + 1), ("onesc", 0, 1)],
               w=[("pA", 0, 512), ("pB", 0, 512), ("pC", 0, 512)])

        for di in range(ND):
            eng = "sync"
            E.op(eng, (lambda di=di: (lambda e: e.dma_start_transpose(
                big_d[:, di * 2048:(di + 1) * 2048],
                xn_h[:, di * 128:(di + 1) * 128])))(),
                r=[("xn_h", 0, T)], w=[("big_d", di * 2048, (di + 1) * 2048)],
                dma=True)

        def gchunk_0():
            E.op("vector", lambda e: e.tensor_scalar_mul(lrow[:, 6:7], pC[0:1, 0:1], 1.0 / T),
                 r=[("pC", 0, 512)], w=[("lrow", 6, 7)])
            E.op("vector", lambda e: e.scalar_tensor_tensor(
                wstr[0:1, 3072:3584], pA[0:1, 0:512], 1.0 / T,
                lrow[0:1, 6:7].broadcast_to((1, 512)), ALU.mult, ALU.subtract),
                r=[("pA", 0, 512), ("lrow", 6, 7)], w=[("wstr", 3072, 3584)])
            E.op("vector", lambda e: e.scalar_tensor_tensor(
                wstr[0:1, 3584:4096], pB[0:1, 0:512], 1.0 / T,
                lrow[0:1, 6:7].broadcast_to((1, 512)), ALU.mult, ALU.subtract),
                r=[("pB", 0, 512), ("lrow", 6, 7)], w=[("wstr", 3584, 4096)])
            for ci in range(8):
                E.op("tensor", (lambda ci=ci: (lambda e: e.matmul(
                    pA[:, 8 + ci:9 + ci], wstr[0:1, 3072 + ci * 128:3072 + (ci + 1) * 128],
                    onesc[0:1, 0:1], start=True, stop=True)))(),
                    r=[("wstr", 3072 + ci * 128, 3072 + (ci + 1) * 128), ("onesc", 0, 1)],
                    w=[("pA", 0, 512)])
            E.op("vector", lambda e: e.tensor_copy(gsm[:, 0:8], pA[:, 8:16]),
                 r=[("pA", 0, 512)], w=[("gsm", 0, 8)])

        def gchunk_1():
            E.op("tensor", [
                (lambda ci=ci: (lambda e: e.matmul(
                    pB[0:1, 128:256], gsm[:, ci:ci + 1], wg1[:, ci * 128:(ci + 1) * 128],
                    start=(ci == 0), stop=(ci == 7))))() for ci in range(8)
            ], r=[("gsm", 0, 8), ("wg1", 0, 1024)], w=[("pB", 0, 512)])
            E.op("vector", lambda e: e.tensor_add(hrow[0:1, :], pB[0:1, 128:256], bg1[0:1, :]),
                 r=[("pB", 0, 512), ("bg1", 0, 4096)], w=[("hrow", 0, 128)])
            E.op("scalar", lambda e: e.activation(hrow[0:1, :], hrow[0:1, :], AF.Relu),
                 r=[("hrow", 0, 128)], w=[("hrow", 0, 128)])

        def gchunk_2():
            E.op("tensor", lambda e: e.matmul(pA[:, 16:17], hrow[0:1, :], onesc[0:1, 0:1],
                                              start=True, stop=True),
                 r=[("hrow", 0, 128), ("onesc", 0, 1)], w=[("pA", 0, 512)])
            E.op("vector", lambda e: e.tensor_copy(gsm[:, 8:9], pA[:, 16:17]),
                 r=[("pA", 0, 512)], w=[("gsm", 8, 9)])
            E.op("tensor", lambda e: e.matmul(pB[0:1, 256:260], gsm[:, 8:9], wg2[:, 0:4],
                                              start=True, stop=True),
                 r=[("gsm", 8, 9), ("wg2", 0, 4)], w=[("pB", 0, 512)])
            E.op("vector", lambda e: e.tensor_add(lrow[0:1, 0:4], pB[0:1, 256:260], bg2[0:1, 0:4]),
                 r=[("pB", 0, 512), ("bg2", 0, 4096)], w=[("lrow", 0, 4)])

        def gchunk_3():
            E.op("vector", lambda e: e.reduce_max(lrow[:, 4:5], lrow[0:1, 0:4], axis=AX.X),
                 r=[("lrow", 0, 4)], w=[("lrow", 4, 5)])
            E.op("tensor", [
                lambda e: e.matmul(pC[0:4, 8:12], onesr[0:1, 0:4], lrow[0:1, 0:4],
                                   start=True, stop=True),
                lambda e: e.matmul(pC[0:4, 12:13], lrow[0:1, 0:4], onesc[0:1, 0:1],
                                   start=True, stop=True),
                lambda e: e.matmul(pC[0:4, 13:14], onesr[0:1, 0:4], lrow[0:1, 4:5],
                                   start=True, stop=True),
            ], r=[("lrow", 0, 5), ("onesr", 0, 128), ("onesc", 0, 1)],
               w=[("pC", 0, 512)])
            E.op("vector", lambda e: e.tensor_copy(grow[0:4, 0:4], pC[0:4, 8:12]),
                 r=[("pC", 0, 512)], w=[("grow", 0, 4)])
            E.op("vector", lambda e: e.tensor_copy(grow[0:4, 4:5], pC[0:4, 12:13]),
                 r=[("pC", 0, 512)], w=[("grow", 4, 5)])
            E.op("vector", lambda e: e.tensor_scalar_mul(grow[0:4, 11:12], pC[0:4, 13:14], -1.0),
                 r=[("pC", 0, 512)], w=[("grow", 11, 12)])
            E.op("vector", lambda e: e.tensor_tensor(
                grow[0:4, 5:9], grow[0:4, 0:4], grow[0:4, 4:5].broadcast_to((4, 4)), ALU.is_gt),
                r=[("grow", 0, 5)], w=[("grow", 5, 9)])
            E.op("vector", lambda e: e.reduce_sum(grow[0:4, 9:10], grow[0:4, 5:9], axis=AX.X),
                 r=[("grow", 5, 9)], w=[("grow", 9, 10)])
            E.op("vector", lambda e: e.tensor_single_scalar(
                grow[0:4, 10:11], grow[0:4, 9:10], 2.0, ALU.is_lt),
                r=[("grow", 9, 10)], w=[("grow", 10, 11)])

        def gchunk_4():
            E.op("scalar", lambda e: e.activation(grow[0:4, 12:13], grow[0:4, 4:5],
                                                  AF.Exp, bias=grow[0:4, 11:12]),
                 r=[("grow", 4, 5), ("grow", 11, 12)], w=[("grow", 12, 13)])
            E.op("vector", lambda e: e.tensor_mul(grow[0:4, 13:14], grow[0:4, 12:13],
                                                  grow[0:4, 10:11]),
                 r=[("grow", 10, 13)], w=[("grow", 13, 14)])
            E.op("tensor", lambda e: e.matmul(pA[0:1, 20:21], grow[0:4, 13:14],
                                              onesc[0:4, 0:1], start=True, stop=True),
                 r=[("grow", 13, 14), ("onesc", 0, 1)], w=[("pA", 0, 512)])
            E.op("vector", lambda e: e.tensor_copy(lrow[:, 5:6], pA[0:1, 20:21]),
                 r=[("pA", 0, 512)], w=[("lrow", 5, 6)])
            E.op("vector", lambda e: e.reciprocal(lrow[:, 6:7], lrow[:, 5:6]),
                 r=[("lrow", 5, 6)], w=[("lrow", 6, 7)])
            E.op("tensor", [
                lambda e: e.matmul(pB[0:1, 300:304], grow[0:4, 13:14], id4[0:4, 0:4],
                                   start=True, stop=True),
                lambda e: e.matmul(pC[0:4, 14:15], onesr[0:1, 0:4], lrow[0:1, 6:7],
                                   start=True, stop=True),
            ], r=[("grow", 13, 14), ("id4", 0, 4), ("onesr", 0, 128), ("lrow", 6, 7)],
               w=[("pB", 0, 512), ("pC", 0, 512)])
            E.op("vector", lambda e: e.tensor_mul(
                lrow[0:1, 8:12], pB[0:1, 300:304], lrow[0:1, 6:7].broadcast_to((1, 4))),
                r=[("pB", 0, 512), ("lrow", 6, 7)], w=[("lrow", 8, 12)])
            E.op("vector", lambda e: e.tensor_copy(lrow[0:1, 12:16], lrow[0:1, 8:12]),
                 r=[("lrow", 8, 12)], w=[("lrow", 12, 16)])
            E.op("vector", lambda e: e.tensor_copy(grow[0:4, 14:15], pC[0:4, 14:15]),
                 r=[("pC", 0, 512)], w=[("grow", 14, 15)])
            E.op("vector", lambda e: e.tensor_mul(grow[0:4, 15:16], grow[0:4, 13:14],
                                                  grow[0:4, 14:15]),
                 r=[("grow", 13, 15)], w=[("grow", 15, 16)])
            E.op("tensor", lambda e: e.matmul(pA[:, 24:28], onesr[0:1, :],
                                              lrow[0:1, 12:16], start=True, stop=True),
                 r=[("onesr", 0, 128), ("lrow", 12, 16)], w=[("pA", 0, 512)])
            E.op("vector", lambda e: e.tensor_copy(gsm[:, 9:13], pA[:, 24:28]),
                 r=[("pA", 0, 512)], w=[("gsm", 9, 13)])

        def gchunk_5():
            for k in range(3):
                for j in range(8):
                    c0 = k * 32 + j * 4
                    o = k * 8 + j
                    E.op("vector", (lambda c0=c0, o=o: (lambda e: e.tensor_mul(
                        bc_cols[:, o:o + 1], bexpT[:, c0:c0 + 1], gsm[:, 9:10])))(),
                        r=[("bexpT", c0, c0 + 1), ("gsm", 9, 13)],
                        w=[("bc_cols", o, o + 1)])
                    for ex in range(1, 4):
                        E.op("vector", (lambda c0=c0, o=o, ex=ex: (lambda e: e.scalar_tensor_tensor(
                            bc_cols[:, o:o + 1], bexpT[:, c0 + ex:c0 + ex + 1],
                            gsm[:, 9 + ex:10 + ex], bc_cols[:, o:o + 1],
                            ALU.mult, ALU.add)))(),
                            r=[("bexpT", c0 + ex, c0 + ex + 1), ("gsm", 9, 13),
                               ("bc_cols", o, o + 1)],
                            w=[("bc_cols", o, o + 1)])

        def gchunk_6():
            E.op("vector", lambda e: e.tensor_copy(bcv_b[:, 0:8], bc_cols[:, 16:24]),
                 r=[("bc_cols", 16, 24)], w=[("bcv_b", 0, 8)])

        gchunks = [gchunk_0, gchunk_1, gchunk_2, gchunk_3, gchunk_4, gchunk_5, gchunk_6]


        psQK = [("pD", pD, 0), ("pE", pE, 0), ("pE", pE, 512),
                ("pF", pF, 0), ("pF", pF, 512)]
        for et in range(16):
            cs = (et % 3) * 1024
            E.op("gpsimd", (lambda et=et, cs=cs: (lambda e: e.dma_start(
                out=big_c[:, cs:cs + 1024],
                in_=wqkvt_h[et].rearrange("p a b -> p (a b)"))))(),
                w=[("big_c", cs, cs + 1024)], dma=True)
            for b in range(NB):
                pname, pt_, off = psQK[(et * NB + b) % 5]
                mms = []
                for di in range(ND):
                    mms.append((lambda cs=cs, di=di, b=b, pt_=pt_, off=off: (lambda e: e.matmul(
                        pt_[:, off:off + 512], big_c[:, cs + di * 128:cs + (di + 1) * 128],
                        big_d[:, di * 2048 + b * 512:di * 2048 + (b + 1) * 512],
                        start=(di == 0), stop=(di == ND - 1))))())
                E.op("tensor", mms,
                     r=[("big_c", cs, cs + 1024), ("big_d", 0, 16384)],
                     w=[(pname, off, off + 512)])
                dst = et * 2048 + b * 512
                E.op("scalar", (lambda pt_=pt_, dst=dst, et=et, off=off: (lambda e: e.activation(
                    big_a[:, dst:dst + 512], pt_[:, off:off + 512], AF.Identity,
                    bias=bqkvc[:, et:et + 1])))(),
                    r=[(pname, off, off + 512), ("bqkvc", et, et + 1)],
                    w=[("big_a", dst, dst + 512)])
            if et >= 2 and et % 2 == 0 and (et - 2) // 2 < len(gchunks):
                gchunks[(et - 2) // 2]()
        for tt in range(NT):
            base = tt * 1040
            E.op("vector", (lambda base=base: (lambda e: e.memset(
                big_b[:, base + 1024:base + 1025], 1.0)))(),
                w=[("big_b", base + 1024, base + 1025)])
        for di in range(ND):
            E.op("scalar", (lambda di=di: (lambda e: e.dma_start(
                out=big_c[:, di * 1024:(di + 1) * 1024],
                in_=wv_nat_h[di * 128:(di + 1) * 128, :])))(),
                w=[("big_c", di * 1024, (di + 1) * 1024)], dma=True)
        for tt in range(NT):
            pv1, pv1n = (pA, "pA") if tt % 2 == 0 else (pC, "pC")
            pv2, pv2n = (pB, "pB") if tt % 2 == 0 else (pD, "pD")
            mms = []
            for di in range(ND):
                for n, pv in ((0, pv1), (1, pv2)):
                    mms.append((lambda di=di, n=n, tt=tt, pv=pv: (lambda e: e.matmul(
                        pv[:, 0:512],
                        big_d[:, di * 2048 + tt * 128:di * 2048 + (tt + 1) * 128],
                        big_c[:, di * 1024 + n * 512:di * 1024 + (n + 1) * 512],
                        start=(di == 0), stop=(di == ND - 1))))())
            E.op("tensor", mms,
                 r=[("big_c", 0, 8192), ("big_d", 0, 16384)],
                 w=[(pv1n, 0, 512), (pv2n, 0, 512)])
            base = tt * 1040
            E.op("vector", (lambda pv1=pv1, base=base: (lambda e: e.tensor_copy(
                big_b[:, base:base + 512], pv1[:, 0:512])))(),
                r=[(pv1n, 0, 512)], w=[("big_b", base, base + 512)])
            E.op("vector", (lambda pv2=pv2, base=base: (lambda e: e.tensor_copy(
                big_b[:, base + 512:base + 1024], pv2[:, 0:512])))(),
                r=[(pv2n, 0, 512)], w=[("big_b", base + 512, base + 1024)])

        def combine_rt(w_h, dst_base, rt):
            acc = fbuf[:, 2048:3072]
            acckey = ("fbuf", 2048, 3072)
            if True:
                for ex in range(4):
                    ws = ex * 1024
                    E.op("gpsimd" if ex % 2 == 0 else "scalar",
                         (lambda w_h=w_h, ex=ex, rt=rt, ws=ws: (lambda e: e.dma_start(
                             out=wstr[:, ws:ws + 1024],
                             in_=w_h[ex, rt * 128:(rt + 1) * 128, :])))(),
                         w=[("wstr", ws, ws + 1024)], dma=True)
                E.op("vector", (lambda: (lambda e: e.tensor_scalar(
                    acc, wstr[:, 0:1024], gsm[:, 9:10], None, ALU.mult)))(),
                    r=[("wstr", 0, 1024), ("gsm", 9, 13)], w=[acckey])
                for ex in (1, 2):
                    ws = ex * 1024
                    E.op("vector", (lambda ws=ws, ex=ex: (lambda e: e.scalar_tensor_tensor(
                        acc, wstr[:, ws:ws + 1024], gsm[:, 9 + ex:10 + ex],
                        acc, ALU.mult, ALU.add)))(),
                        r=[("wstr", ws, ws + 1024), ("gsm", 9, 13), acckey],
                        w=[acckey])
                dst = dst_base + rt * 1024
                E.op("vector", (lambda dst=dst: (lambda e: e.scalar_tensor_tensor(
                    big_c[:, dst:dst + 1024], wstr[:, 3072:4096],
                    gsm[:, 12:13], acc, ALU.mult, ALU.add)))(),
                    r=[("wstr", 3072, 4096), ("gsm", 9, 13), acckey],
                    w=[("big_c", dst, dst + 1024)])

        def combine(proj_idx, w_h, dst_base):
            for rt in range(8):
                combine_rt(w_h, dst_base, rt)


        for bi, b in enumerate([3, 2, 1, 0]):
            nkt = 4 * b + 4
            for kt in range(nkt):
                pname, pS = ("pA", pA) if kt % 2 == 0 else ("pB", pB)
                diag = kt >= 4 * b
                mms = []
                for di in range(ND):
                    mms.append((lambda di=di, kt=kt, b=b, pS=pS: (lambda e: e.matmul(
                        pS[:, 0:512],
                        big_a[:, 16384 + di * 2048 + kt * 128:16384 + di * 2048 + (kt + 1) * 128],
                        big_a[:, di * 2048 + b * 512:di * 2048 + (b + 1) * 512],
                        start=(di == 0), stop=(di == ND - 1) and not diag)))())
                rr = [("big_a", 0, 32768)]
                if diag:
                    idx = kt - 4 * b
                    mms.append((lambda idx=idx, pS=pS: (lambda e: e.matmul(
                        pS[:, 0:512], negid[:, 0:128],
                        masks[:, idx * 512:(idx + 1) * 512],
                        start=False, stop=True)))())
                    rr += [("negid", 0, 128), ("masks", idx * 512, (idx + 1) * 512)]
                E.op("tensor", mms, r=rr, w=[(pname, 0, 512)])
                pdst = 8192 + kt * 512
                E.op("scalar", (lambda pS=pS, pdst=pdst: (lambda e: e.activation(
                    big_c[:, pdst:pdst + 512], pS[:, 0:512], AF.Exp,
                    scale=1.0 / 32.0)))(),
                    r=[(pname, 0, 512)], w=[("big_c", pdst, pdst + 512)])
            for qt in range(4):
                Q = 4 * b + qt
                pY = pE if qt % 2 == 0 else pF
                pYn = "pE" if qt % 2 == 0 else "pF"
                pDen, pDenN = (pC, "pC") if qt % 2 == 0 else (pD, "pD")
                mms = []
                for kt in range(Q + 1):
                    psrc = 8192 + kt * 512 + qt * 128
                    for n in range(2):
                        mms.append((lambda kt=kt, n=n, psrc=psrc, pY=pY, Q=Q: (lambda e: e.matmul(
                            pY[:, n * 512:(n + 1) * 512],
                            big_c[:, psrc:psrc + 128],
                            big_b[:, kt * 1040 + n * 512:kt * 1040 + (n + 1) * 512],
                            start=(kt == 0), stop=(kt == Q))))())
                    mms.append((lambda kt=kt, psrc=psrc, pDen=pDen, Q=Q: (lambda e: e.matmul(
                        pDen[:, 0:1], big_c[:, psrc:psrc + 128],
                        big_b[:, kt * 1040 + 1024:kt * 1040 + 1025],
                        start=(kt == 0), stop=(kt == Q))))())
                E.op("tensor", mms,
                     r=[("big_c", 8192 + kt2 * 512, 8192 + (kt2 + 1) * 512)
                        for kt2 in range(Q + 1)] + [("big_b", 0, 16640)],
                     w=[(pYn, 0, 1024), (pDenN, 0, 512)])
                E.op("vector", (lambda pDen=pDen, qt=qt: (lambda e: e.reciprocal(
                    rc[:, qt:qt + 1], pDen[:, 0:1])))(),
                    r=[(pDenN, 0, 512)], w=[("rc", qt, qt + 1)])
                ya = fbuf[:, (qt % 2) * 1024:(qt % 2) * 1024 + 1024]
                yak = ("fbuf", (qt % 2) * 1024, (qt % 2) * 1024 + 1024)
                E.op("scalar", (lambda pY=pY, ya=ya, qt=qt: (lambda e: e.activation(
                    ya, pY[:, 0:1024], AF.Copy, scale=rc[:, qt:qt + 1])))(),
                    r=[(pYn, 0, 1024), ("rc", qt, qt + 1)], w=[yak])
                E.op("gpsimd", (lambda ya=ya, Q=Q: (lambda e: e.dma_start(
                    out=ya, in_=x_h[Q * 128:(Q + 1) * 128, :],
                    accum_op=mybir.AluOpType.add)))(),
                    r=[yak], w=[yak], dma=True)
                E.op("sync", (lambda ya=ya, Q=Q: (lambda e: e.dma_start(
                    out=ya_h[Q * 128:(Q + 1) * 128, :], in_=ya)))(),
                    r=[yak], w=[("ya_h", Q * 128, (Q + 1) * 128)], dma=True)
            for (wh, base, rt) in (
                ((wq_e_h, 0, 0), (wq_e_h, 0, 1), (wq_e_h, 0, 2),
                 (wk_e_h, 8192, 6), (wk_e_h, 8192, 7)),
                ((wq_e_h, 0, 3), (wq_e_h, 0, 4), (wq_e_h, 0, 5),
                 (wk_e_h, 8192, 4), (wk_e_h, 8192, 5)),
                ((wq_e_h, 0, 6), (wq_e_h, 0, 7),
                 (wk_e_h, 8192, 2), (wk_e_h, 8192, 3)),
                ((wk_e_h, 8192, 0), (wk_e_h, 8192, 1)),
            )[bi]:
                combine_rt(wh, base, rt)

        for dst_off, wc_base in ((16384, 0), (0, 8192)):
            bias_off = 0 if dst_off == 16384 else 8
            for ci in range(8):
                for b in range(NB):
                    pname, pt_, off = psQK[(ci * NB + b) % 5]
                    mms = []
                    for di in range(ND):
                        mms.append((lambda di=di, ci=ci, b=b, pt_=pt_, off=off, wc_base=wc_base: (lambda e: e.matmul(
                            pt_[:, off:off + 512],
                            big_c[:, wc_base + di * 1024 + ci * 128:wc_base + di * 1024 + (ci + 1) * 128],
                            big_d[:, di * 2048 + b * 512:di * 2048 + (b + 1) * 512],
                            start=(di == 0), stop=(di == ND - 1))))())
                    E.op("tensor", mms,
                         r=[("big_c", wc_base, wc_base + 8192), ("big_d", 0, 16384)],
                         w=[(pname, off, off + 512)])
                    dst = dst_off + ci * 2048 + b * 512
                    bi = bias_off + ci
                    E.op("scalar", (lambda pt_=pt_, dst=dst, bi=bi, off=off: (lambda e: e.activation(
                        big_a[:, dst:dst + 512], pt_[:, off:off + 512], AF.Identity,
                        bias=bc_cols[:, bi:bi + 1])))(),
                        r=[(pname, off, off + 512), ("bc_cols", bi, bi + 1)],
                        w=[("big_a", dst, dst + 512)])
            if dst_off == 16384:
                combine(2, wv_e_h, 0)
        for tt in range(NT):
            pv, pvn = (pE, "pE") if tt % 2 == 0 else (pF, "pF")
            mms = []
            for n in range(2):
                for di in range(ND):
                    mms.append((lambda di=di, n=n, tt=tt, pv=pv: (lambda e: e.matmul(
                        pv[:, n * 512:(n + 1) * 512],
                        big_d[:, di * 2048 + tt * 128:di * 2048 + (tt + 1) * 128],
                        big_c[:, di * 1024 + n * 512:di * 1024 + (n + 1) * 512],
                        start=(di == 0), stop=(di == ND - 1))))())
            E.op("tensor", mms,
                 r=[("big_c", 0, 8192), ("big_d", 0, 16384)], w=[(pvn, 0, 1024)])
            base = tt * 1040
            pv_v = pv[:, 0:1024].rearrange("p (h d) -> p h d", d=64)
            dst_v = big_b[:, base:base + 1040].rearrange("p (h d) -> p h d", d=65)[:, :, 0:64]
            E.op("vector", (lambda pv_v=pv_v, dst_v=dst_v: (lambda e: e.tensor_copy(
                dst_v, pv_v)))(),
                r=[(pvn, 0, 1024)], w=[("big_b", base, base + 1040)])
            ones_v = big_b[:, base:base + 1040].rearrange("p (h d) -> p h d", d=65)[:, :, 64:65]
            E.op("vector", (lambda ones_v=ones_v: (lambda e: e.memset(ones_v, 1.0)))(),
                 r=[], w=[("big_b", base, base + 1040)])

        for di in range(ND):
            E.op("gpsimd", (lambda di=di: (lambda e: e.dma_start(
                out=big_c[:, di * 1024:(di + 1) * 1024],
                in_=wo_h[di * 128:(di + 1) * 128, :])))(),
                w=[("big_c", di * 1024, (di + 1) * 1024)], dma=True)
        mms = []
        for di in range(ND):
            for n in range(2):
                ptgt = pC if n == 0 else pA
                mms.append((lambda di=di, n=n, ptgt=ptgt: (lambda e: e.matmul(
                    ptgt[0:1, 0:512], bcv_b[:, di:di + 1],
                    big_c[:, di * 1024 + n * 512:di * 1024 + (n + 1) * 512],
                    start=(di == 0), stop=(di == ND - 1))))())
        E.op("tensor", mms, r=[("bcv_b", 0, 8), ("big_c", 0, 8192)],
             w=[("pC", 0, 512), ("pA", 0, 512)])
        E.op("vector", lambda e: e.tensor_add(borow[0:1, 0:512], pC[0:1, 0:512],
                                              borow[0:1, 0:512]),
             r=[("pC", 0, 512), ("borow", 0, 4096)], w=[("borow", 0, 2048)])
        E.op("vector", lambda e: e.tensor_add(borow[0:1, 512:1024], pA[0:1, 0:512],
                                              borow[0:1, 512:1024]),
             r=[("pA", 0, 512), ("borow", 0, 4096)], w=[("borow", 2048, 4096)])
        E.op("vector", lambda e: e.tensor_add(borow[0:1, 0:1024], borow[0:1, 0:1024],
                                              bvrow[0:1, 0:1024]),
             r=[("borow", 0, 4096), ("bvrow", 0, 4096)], w=[("borow", 0, 4096)])

        norm_q = []
        pv_q = []

        def _normalize(pY, pYn, rows, rs, dst, hs):
            rb0 = 0 if rows[0] == 0 else 512
            E.op("gpsimd", (lambda rb0=rb0, hs=hs: (lambda e: e.dma_start(
                out=fbuf[0:64, rb0:rb0 + 512],
                in_=nrm_h[hs:hs + 1, :].broadcast_to((64, 512)))))(),
                r=[("nrm_scr", hs, hs + 1)], w=[("fbuf", rb0, rb0 + 512)],
                dma=True)
            E.op("vector", (lambda pY=pY, rb0=rb0, dst=dst, rows=rows: (lambda e: e.tensor_tensor(
                big_d[rows[0]:rows[1], dst:dst + 512], pY[0:64, 0:512],
                fbuf[0:64, rb0:rb0 + 512], ALU.mult)))(),
                r=[(pYn, 0, 512), ("fbuf", rb0, rb0 + 512)],
                w=[("big_d", dst, dst + 512)])

        for b in range(NB):
            for pr in range(8):
                ci = pr
                hA, hB = 2 * pr, 2 * pr + 1
                pva, pvan = (pA, "pA") if pr % 2 == 0 else (pC, "pC")
                pvb, pvbn = (pB, "pB") if pr % 2 == 0 else (pD, "pD")
                for kt in range(16):
                    base = (kt % 2) * 1024
                    slot = 8192 + (kt % 8) * 1024
                    sA, sB = slot, slot + 512
                    pSa = pEF[:, base:base + 512]
                    pSb = pEF[:, base + 512:base + 1024]
                    kA = ("pEF", base, base + 512)
                    kB = ("pEF", base + 512, base + 1024)
                    exA = [("pE", 0, 512)] if kt == 0 else (
                        [("pF", 0, 512)] if kt == 1 else [])
                    exB = [("pE", 512, 1024)] if kt == 0 else (
                        [("pF", 512, 1024)] if kt == 1 else [])
                    E.op("tensor", (lambda ci=ci, kt=kt, b=b, pSa=pSa: (lambda e: e.matmul(
                        pSa, big_a[0:64, ci * 2048 + kt * 128:ci * 2048 + (kt + 1) * 128],
                        big_a[0:64, 16384 + ci * 2048 + b * 512:16384 + ci * 2048 + (b + 1) * 512],
                        start=True, stop=True, tile_position=(0, 0))))(),
                        r=[("big_a", 0, 32768)], w=[kA] + exA)
                    E.op("tensor", (lambda ci=ci, kt=kt, b=b, pSb=pSb: (lambda e: e.matmul(
                        pSb, big_a[64:128, ci * 2048 + kt * 128:ci * 2048 + (kt + 1) * 128],
                        big_a[64:128, 16384 + ci * 2048 + b * 512:16384 + ci * 2048 + (b + 1) * 512],
                        start=True, stop=True, tile_position=(64, 0))))(),
                        r=[("big_a", 0, 32768)], w=[kB] + exB)
                    E.op("scalar", (lambda base=base, slot=slot: (lambda e: e.activation(
                        big_c[:, slot:slot + 1024], pEF[:, base:base + 1024],
                        AF.Exp, scale=0.125)))(),
                        r=[("pEF", base, base + 1024)],
                        w=[("big_c", slot, slot + 1024)])
                    def _pv(kt=kt, hA=hA, hB=hB, sA=sA, sB=sB, pva=pva, pvb=pvb,
                            pvan=pvan, pvbn=pvbn):
                        E.op("tensor", (lambda: (lambda e: e.matmul(
                            pva[0:65, 0:512],
                            big_b[:, kt * 1040 + hA * 65:kt * 1040 + hA * 65 + 65],
                            big_c[:, sA:sA + 512],
                            start=(kt == 0), stop=(kt == 15))))(),
                            r=[("big_c", sA, sA + 512), ("big_b", 0, 16640)],
                            w=[(pvan, 0, 512)])
                        E.op("tensor", (lambda: (lambda e: e.matmul(
                            pvb[0:65, 0:512],
                            big_b[:, kt * 1040 + hB * 65:kt * 1040 + hB * 65 + 65],
                            big_c[:, sB:sB + 512],
                            start=(kt == 0), stop=(kt == 15))))(),
                            r=[("big_c", sB, sB + 512), ("big_b", 0, 16640)],
                            w=[(pvbn, 0, 512)])
                    pv_q.append(_pv)
                    if len(pv_q) > 2:
                        pv_q.pop(0)()
                    if kt == 8:
                        while norm_q:
                            _normalize(*norm_q.pop(0))

                def _tail(pva=pva, pvan=pvan, pvb=pvb, pvbn=pvbn,
                          pr=pr, b=b, dst=ci * 2048 + b * 512):
                    for hh, (pY, pYn, rows, rs) in enumerate((
                            (pva, pvan, (0, 64), 1024 + (pr % 2) * 512),
                            (pvb, pvbn, (64, 128), 2048 + (pr % 2) * 512))):
                        hs = 2 * pr + hh
                        E.op("vector", (lambda pY=pY, rs=rs: (lambda e: e.reciprocal(
                            fbuf[0:1, rs:rs + 512], pY[64:65, 0:512])))(),
                            r=[(pYn, 0, 512)], w=[("fbuf", rs, rs + 512)])
                        E.op("sync", (lambda rs=rs, hs=hs: (lambda e: e.dma_start(
                            out=nrm_h[hs:hs + 1, :], in_=fbuf[0:1, rs:rs + 512])))(),
                            r=[("fbuf", rs, rs + 512)],
                            w=[("nrm_scr", hs, hs + 1)], dma=True)
                        norm_q.append((pY, pYn, rows, rs, dst, hs))
                for _f in pv_q:
                    _f()
                pv_q.clear()
                pv_q.append(_tail)

        while pv_q:
            pv_q.pop(0)()
        while norm_q:
            _normalize(*norm_q.pop(0))

        out_toks = []
        for tt in range(NT):
            pY, pYn = (pE, "pE") if tt % 2 == 0 else (pF, "pF")
            ys = (tt % 4) * 1024
            qeng = "sync" if tt % 2 == 0 else "scalar"
            E.op(qeng, (lambda tt=tt, ys=ys: (lambda e: e.dma_start(
                out=wstr[:, ys:ys + 1024], in_=ya_h[tt * 128:(tt + 1) * 128, :])))(),
                r=[("ya_h", tt * 128, (tt + 1) * 128)],
                w=[("wstr", ys, ys + 1024)], dma=True)
            mms = []
            for di in range(ND):
                for n in range(2):
                    mms.append((lambda di=di, n=n, tt=tt, pY=pY: (lambda e: e.matmul(
                        pY[:, n * 512:(n + 1) * 512],
                        big_d[:, di * 2048 + tt * 128:di * 2048 + (tt + 1) * 128],
                        big_c[:, di * 1024 + n * 512:di * 1024 + (n + 1) * 512],
                        start=(di == 0), stop=False)))())
            for n in range(2):
                mms.append((lambda n=n, pY=pY: (lambda e: e.matmul(
                    pY[:, n * 512:(n + 1) * 512], onesr[0:1, 0:128],
                    borow[0:1, n * 512:(n + 1) * 512],
                    start=False, stop=True)))())
            exEF = [("pEF", (tt % 2) * 1024, (tt % 2) * 1024 + 1024)] if tt < 2 else []
            E.op("tensor", mms, r=[("big_d", 0, 16384), ("big_c", 0, 8192),
                                   ("borow", 0, 4096), ("onesr", 0, 128)],
                 w=[(pYn, 0, 1024)] + exEF)
            E.op("vector", (lambda pY=pY, ys=ys: (lambda e: e.tensor_add(
                wstr[:, ys:ys + 1024], pY[:, 0:1024], wstr[:, ys:ys + 1024])))(),
                r=[(pYn, 0, 1024), ("wstr", ys, ys + 1024)],
                w=[("wstr", ys, ys + 1024)])
            tok = E.op("gpsimd", (lambda tt=tt, ys=ys: (lambda e: e.dma_start(
                out=out_h[tt * 128:(tt + 1) * 128, :], in_=wstr[:, ys:ys + 1024])))(),
                r=[("wstr", ys, ys + 1024)], w=[("out_h", tt * 128, (tt + 1) * 128)],
                dma=True)
            out_toks.append(tok)
        E.barrier("sync", out_toks)

        with nc.Block() as block:
            handles = {}

            @block.sync
            def _(eng):
                handles["sync"] = eng
                for waits, fns, inc in E.ops["sync"]:
                    for sname, val in waits:
                        eng.wait_ge(sems[sname], val)
                    inst = None
                    for fn in fns:
                        inst = fn(eng)
                    if inc is not None and inst is not None:
                        inst.then_inc(sems[inc[0]], inc[1])

            @block.tensor
            def _(eng):
                for waits, fns, inc in E.ops["tensor"]:
                    for sname, val in waits:
                        eng.wait_ge(sems[sname], val)
                    inst = None
                    for fn in fns:
                        inst = fn(eng)
                    if inc is not None and inst is not None:
                        inst.then_inc(sems[inc[0]], inc[1])

            @block.scalar
            def _(eng):
                for waits, fns, inc in E.ops["scalar"]:
                    for sname, val in waits:
                        eng.wait_ge(sems[sname], val)
                    inst = None
                    for fn in fns:
                        inst = fn(eng)
                    if inc is not None and inst is not None:
                        inst.then_inc(sems[inc[0]], inc[1])

            @block.vector
            def _(eng):
                for waits, fns, inc in E.ops["vector"]:
                    for sname, val in waits:
                        eng.wait_ge(sems[sname], val)
                    inst = None
                    for fn in fns:
                        inst = fn(eng)
                    if inc is not None and inst is not None:
                        inst.then_inc(sems[inc[0]], inc[1])

            @block.gpsimd
            def _(eng):
                for waits, fns, inc in E.ops["gpsimd"]:
                    for sname, val in waits:
                        eng.wait_ge(sems[sname], val)
                    inst = None
                    for fn in fns:
                        inst = fn(eng)
                    if inc is not None and inst is not None:
                        inst.then_inc(sems[inc[0]], inc[1])

    return nc


_NC_CACHE = None
_LAST_E = None


def kernel(**inputs):
    global _NC_CACHE
    if _NC_CACHE is None:
        _NC_CACHE = build_nc()
    nc = _NC_CACHE
    common = _host_prep(inputs)
    x = np.ascontiguousarray(np.asarray(inputs["L0_attn_x"], np.float32))
    in_maps = []
    for b in range(8):
        m = {"x": np.ascontiguousarray(x[b])}
        m.update(common)
        in_maps.append(m)
    res = run_bass_kernel_spmd(nc, in_maps, core_ids=list(range(8)))
    return np.stack([np.asarray(r["out"], np.float32) for r in res.results], axis=0)


# version 24 (final)
# speedup vs baseline: 1.0809x; 1.0809x over previous
import numpy as np
import ml_dtypes
from contextlib import ExitStack

import concourse.bass as bass
import concourse.mybir as mybir
from concourse.bass_utils import run_bass_kernel_spmd

F32 = mybir.dt.float32
BF16 = mybir.dt.bfloat16
AF = mybir.ActivationFunctionType
ALU = mybir.AluOpType
AX = mybir.AxisListType
bf16 = ml_dtypes.bfloat16

T, D, E3 = 2048, 1024, 3072
NT, ND, NB, NH = 16, 8, 4, 16
INV_D = 1.0 / 1024.0
LN_EPS = 1e-5


class _Emitter:

    ENGINES = ("sync", "tensor", "scalar", "vector", "gpsimd")
    DMAQ = {"sync": "dq_s", "gpsimd": "dq_g", "scalar": "dq_a"}
    RING = 12

    def __init__(self):
        self.ops = {e: [] for e in self.ENGINES}
        self.count = {}
        self.water = {}
        self.regions = {}
        self.dma_idx = {q: 0 for q in self.DMAQ.values()}
        self.log = []
        for e in self.ENGINES:
            self.count[e] = 0
        for q in self.DMAQ.values():
            for k in range(self.RING):
                self.count[f"{q}_{k}"] = 0

    def sem_names(self):
        names = list(self.ENGINES)
        for q in self.DMAQ.values():
            names += [f"{q}_{k}" for k in range(self.RING)]
        return names

    def _deps_for(self, reads, writes):
        deps = []
        for key, lo, hi in reads:
            for seg in self.regions.get(key, ()):
                if seg[0] < hi and lo < seg[1] and seg[2] is not None:
                    deps.append(seg[2])
        for key, lo, hi in writes:
            for seg in self.regions.get(key, ()):
                if seg[0] < hi and lo < seg[1]:
                    if seg[2] is not None:
                        deps.append(seg[2])
                    deps.extend(seg[3].values())
        return deps

    def _record(self, reads, writes, tok):
        for key, lo, hi in reads:
            segs = self.regions.setdefault(key, [])
            hit = False
            for seg in segs:
                if seg[0] == lo and seg[1] == hi:
                    seg[3][tok[0]] = tok
                    hit = True
            if not hit:
                segs.append([lo, hi, None, {tok[0]: tok}])
        for key, lo, hi in writes:
            segs = self.regions.setdefault(key, [])
            keep = []
            for seg in segs:
                if seg[0] < hi and lo < seg[1]:
                    if seg[0] < lo:
                        keep.append([seg[0], lo, seg[2], dict(seg[3])])
                    if hi < seg[1]:
                        keep.append([hi, seg[1], seg[2], dict(seg[3])])
                else:
                    keep.append(seg)
            keep.append([lo, hi, tok, {}])
            self.regions[key] = keep

    def _emit_waits(self, eng, deps):
        waits = []
        best = {}
        for sem, val in deps:
            if val > best.get(sem, 0):
                best[sem] = val
        for sem, val in best.items():
            if self.water.get((eng, sem), 0) >= val:
                continue
            self.water[(eng, sem)] = val
            waits.append((sem, val))
        return waits

    def op(self, eng, fns, r=(), w=(), dma=False, extra_deps=()):
        if not isinstance(fns, (list, tuple)):
            fns = [fns]
        deps = self._deps_for(r, w)
        deps.extend(extra_deps)
        if dma:
            q = self.DMAQ[eng]
            sem = f"{q}_{self.dma_idx[q] % self.RING}"
            self.dma_idx[q] += 1
            if self.count[sem] > 0:
                deps.append((sem, self.count[sem]))
            waits = self._emit_waits(eng, deps)
            self.count[sem] += 16
            tok = (sem, self.count[sem])
            inc = (sem, 16)
        else:
            waits = self._emit_waits(eng, deps)
        if not dma:
            self.count[eng] += 1
            tok = (eng, self.count[eng])
            inc = (eng, 1)
        self.ops[eng].append((waits, fns, inc))
        self.log.append((eng, tuple(r), tuple(w), tok))
        self._record(r, w, tok)
        return tok

    def barrier(self, eng, toks):
        waits = self._emit_waits(eng, list(toks))
        if waits:
            self.ops[eng].append((waits, [], None))

    def replay(self, nc, handles, sems):
        for eng_name in self.ENGINES:
            oplist = self.ops[eng_name]
            h = handles[eng_name]
            for waits, fns, inc in oplist:
                for sem, val in waits:
                    h.wait_ge(sems[sem], val)
                inst = None
                for fn in fns:
                    inst = fn(h)
                if inc is not None and inst is not None:
                    inst.then_inc(sems[inc[0]], inc[1])


def _host_prep(inp):
    f = np.float32
    Wqkv = np.asarray(inp["L0_attn_W_qkv"], f)
    bqkv = np.asarray(inp["L0_attn_b_qkv"], f)
    d = {}
    d["wqkvt"] = np.ascontiguousarray(
        Wqkv.reshape(24, 128, 8, 128).transpose(0, 3, 2, 1)[:16]).astype(bf16)
    d["wv_nat"] = np.ascontiguousarray(Wqkv[2048:3072].T).astype(bf16)
    d["bqkv_cols"] = np.ascontiguousarray(bqkv.reshape(24, 128).T)
    d["bv_row"] = np.ascontiguousarray(bqkv[2048:].reshape(1, 1024))
    for nm, key in (("wq_e", "L0_ffn_Wq"), ("wk_e", "L0_ffn_Wk"),
                    ("wv_e", "L0_ffn_Wv")):
        d[nm] = np.ascontiguousarray(np.asarray(inp[key], f))
    bexp = []
    for key in ("L0_ffn_bq", "L0_ffn_bk", "L0_ffn_bv"):
        b = np.asarray(inp[key], f)
        bexp.append(b.reshape(4, 8, 128).transpose(2, 1, 0).reshape(128, 32))
    d["bexpT"] = np.ascontiguousarray(np.concatenate(bexp, axis=1))
    Wg1 = np.asarray(inp["L0_ffn_Wg1"], f)
    d["wg1t"] = np.ascontiguousarray(
        Wg1.reshape(8, 128, 128).transpose(1, 0, 2).reshape(128, 1024))
    d["wg2"] = np.ascontiguousarray(np.asarray(inp["L0_ffn_Wg2"], f))
    d["bg1_row"] = np.asarray(inp["L0_ffn_bg1"], f).reshape(1, 128)
    d["bg2_row"] = np.asarray(inp["L0_ffn_bg2"], f).reshape(1, 4)
    d["wo_b"] = np.ascontiguousarray(np.asarray(inp["L0_ffn_Wo"], f)).astype(bf16)
    d["bo_row"] = np.asarray(inp["L0_ffn_bo"], f).reshape(1, 1024)
    masks = np.zeros((128, 2048), dtype=bf16)
    for idx in range(4):
        p = np.arange(128)[:, None]
        n = np.arange(512)[None, :]
        masks[:, idx * 512:(idx + 1) * 512] = (p > n - 128 * idx).astype(bf16)
    d["masks"] = masks
    d["negid"] = (np.eye(128, dtype=np.float32) * -1e9).astype(bf16)
    d["onesf_row"] = np.ones((1, 128), f)
    d["onesf_col"] = np.ones((128, 1), f)
    d["id4"] = np.eye(4, dtype=f)
    return d


def build_nc():
    nc = bass.Bass("TRN2", target_bir_lowering=False, debug=False)

    x_h = nc.dram_tensor("x", [T, D], F32, kind="ExternalInput")
    wqkvt_h = nc.dram_tensor("wqkvt", [16, 128, 8, 128], BF16, kind="ExternalInput")
    wv_nat_h = nc.dram_tensor("wv_nat", [1024, 1024], BF16, kind="ExternalInput")
    bqkvc_h = nc.dram_tensor("bqkv_cols", [128, 24], F32, kind="ExternalInput")
    bvrow_h = nc.dram_tensor("bv_row", [1, 1024], F32, kind="ExternalInput")
    wq_e_h = nc.dram_tensor("wq_e", [4, 1024, 1024], F32, kind="ExternalInput")
    wk_e_h = nc.dram_tensor("wk_e", [4, 1024, 1024], F32, kind="ExternalInput")
    wv_e_h = nc.dram_tensor("wv_e", [4, 1024, 1024], F32, kind="ExternalInput")
    bexpT_h = nc.dram_tensor("bexpT", [128, 96], F32, kind="ExternalInput")
    wg1t_h = nc.dram_tensor("wg1t", [128, 1024], F32, kind="ExternalInput")
    wg2_h = nc.dram_tensor("wg2", [128, 4], F32, kind="ExternalInput")
    bg1_h = nc.dram_tensor("bg1_row", [1, 128], F32, kind="ExternalInput")
    bg2_h = nc.dram_tensor("bg2_row", [1, 4], F32, kind="ExternalInput")
    wo_h = nc.dram_tensor("wo_b", [1024, 1024], BF16, kind="ExternalInput")
    bo_h = nc.dram_tensor("bo_row", [1, 1024], F32, kind="ExternalInput")
    masks_h = nc.dram_tensor("masks", [128, 2048], BF16, kind="ExternalInput")
    negid_h = nc.dram_tensor("negid", [128, 128], BF16, kind="ExternalInput")
    nrm_h = nc.dram_tensor("nrm_scr", [16, 512], F32, kind="Internal")
    ones_r_h = nc.dram_tensor("onesf_row", [1, 128], F32, kind="ExternalInput")
    ones_c_h = nc.dram_tensor("onesf_col", [128, 1], F32, kind="ExternalInput")
    id4_h = nc.dram_tensor("id4", [4, 4], F32, kind="ExternalInput")

    xn_h = nc.dram_tensor("xn_scr", [T, D], BF16, kind="Internal")
    ya_h = nc.dram_tensor("ya_scr", [T, D], F32, kind="Internal")
    out_h = nc.dram_tensor("out", [T, D], F32, kind="ExternalOutput")

    E = _Emitter()
    global _LAST_E
    _LAST_E = E

    with ExitStack() as st:
        def sb(name, shape, dt):
            return st.enter_context(nc.sbuf_tensor(name, shape, dt))

        def ps(name, shape):
            return st.enter_context(nc.psum_tensor(name, shape, F32))

        big_a = sb("big_a", [128, 32768], BF16)
        big_b = sb("big_b", [128, 16640], BF16)
        big_c = sb("big_c", [128, 16384], BF16)
        big_d = sb("big_d", [128, 16384], BF16)
        wstr = sb("wstr", [128, 4096], F32)
        fbuf = sb("fbuf", [128, 3072], F32)
        masks = sb("masks_s", [128, 2048], BF16)
        negid = sb("negid_s", [128, 128], BF16)
        wg1 = sb("wg1_s", [128, 1024], F32)
        bexpT = sb("bexpT_s", [128, 96], F32)
        bqkvc = sb("bqkvc_s", [128, 24], F32)
        bvrow = sb("bvrow_s", [1, 1024], F32)
        borow = sb("borow_s", [1, 1024], F32)
        onesr = sb("onesr_s", [1, 128], F32)
        onesc = sb("onesc_s", [128, 1], F32)
        id4 = sb("id4_s", [4, 4], F32)
        wg2 = sb("wg2_s", [128, 4], F32)
        bg1 = sb("bg1_s", [1, 128], F32)
        bg2 = sb("bg2_s", [1, 4], F32)
        st_r = sb("st_r", [128, 16], F32)
        st_sum = sb("st_sum", [128, 16], F32)
        st_sq = sb("st_sq", [128, 16], F32)
        st_m = sb("st_m", [128, 16], F32)
        st_v = sb("st_v", [128, 16], F32)
        st_rm = sb("st_rm", [128, 16], F32)
        st_nmr = sb("st_nmr", [128, 16], F32)
        rc = sb("rc", [128, 4], F32)
        gsm = sb("gsm", [128, 40], F32)
        bc_cols = sb("bc_cols", [128, 24], F32)
        bcv_b = sb("bcv_b", [128, 8], BF16)
        grow = sb("grow", [4, 48], F32)
        hrow = sb("hrow", [1, 128], F32)
        lrow = sb("lrow", [1, 16], F32)

        pA = ps("pA", [128, 512])
        pB = ps("pB", [128, 512])
        pC = ps("pC", [128, 512])
        pD = ps("pD", [128, 512])
        pEF = ps("pEF", [128, 2048])
        pE = pEF[:, 0:1024]
        pF = pEF[:, 1024:2048]

        sems = {}
        for s in E.sem_names():
            sems[s] = st.enter_context(nc.semaphore(f"sem_{s}"))

        def bslice(tensor, c0, c1, p0=0, p1=128):
            return tensor[p0:p1, c0:c1]

        for name, hbm, sbuf_t, cols in (
            ("masks", masks_h, masks, 2048), ("negid", negid_h, negid, 128),
            ("wg1", wg1t_h, wg1, 1024),
            ("bexpT", bexpT_h, bexpT, 96), ("bqkvc", bqkvc_h, bqkvc, 24),
            ("wg2", wg2_h, wg2, 4), ("onesc", ones_c_h, onesc, 1),
        ):
            E.op("sync", (lambda h=hbm, s=sbuf_t: (lambda e: e.dma_start(out=s[:, :], in_=h[:, :])))(),
                 w=[(name, 0, cols)], dma=True)
        for name, hbm, sbuf_t in (
            ("bvrow", bvrow_h, bvrow), ("borow", bo_h, borow),
            ("onesr", ones_r_h, onesr), ("bg1", bg1_h, bg1), ("bg2", bg2_h, bg2),
        ):
            E.op("sync", (lambda h=hbm, s=sbuf_t: (lambda e: e.dma_start(out=s[:, :], in_=h[:, :])))(),
                 w=[(name, 0, 4096)], dma=True)
        E.op("sync", lambda e: e.dma_start(out=id4[:, :], in_=id4_h[:, :]),
             w=[("id4", 0, 4)], dma=True)

        XS = [0, 1024]
        for tt in range(NT):
            s = XS[tt % 2]
            xa = fbuf[:, s:s + 1024]
            E.op("scalar", (lambda tt=tt, xa=xa: (lambda e: e.dma_start(
                out=xa, in_=x_h[tt * 128:(tt + 1) * 128, :])))(),
                r=[], w=[("fbuf", s, s + 1024)], dma=True)
            E.op("scalar", (lambda xa=xa, tt=tt: (lambda e: e.activation(
                fbuf[:, 2048:3072], xa, AF.Square,
                accum_out=st_sq[:, tt:tt + 1])))(),
                r=[("fbuf", s, s + 1024)],
                w=[("fbuf", 2048, 3072), ("st_sq", tt, tt + 1)])
            E.op("vector", (lambda xa=xa, tt=tt: (lambda e: e.reduce_sum(
                st_sum[:, tt:tt + 1], xa, axis=AX.X)))(),
                r=[("fbuf", s, s + 1024)], w=[("st_sum", tt, tt + 1)])
        E.op("vector", lambda e: e.tensor_scalar_mul(st_m[:, :], st_sum[:, :], INV_D),
             r=[("st_sum", 0, 16)], w=[("st_m", 0, 16)])
        E.op("vector", lambda e: e.tensor_scalar_mul(st_v[:, :], st_sq[:, :], INV_D),
             r=[("st_sq", 0, 16)], w=[("st_v", 0, 16)])
        E.op("scalar", lambda e: e.activation(st_sq[:, :], st_m[:, :], AF.Square),
             r=[("st_m", 0, 16)], w=[("st_sq", 0, 16)])
        E.op("vector", lambda e: e.tensor_sub(st_v[:, :], st_v[:, :], st_sq[:, :]),
             r=[("st_v", 0, 16), ("st_sq", 0, 16)], w=[("st_v", 0, 16)])
        E.op("vector", lambda e: e.tensor_scalar_add(st_v[:, :], st_v[:, :], LN_EPS),
             r=[("st_v", 0, 16)], w=[("st_v", 0, 16)])
        E.op("scalar", lambda e: e.activation(st_sq[:, :], st_v[:, :], AF.Ln),
             r=[("st_v", 0, 16)], w=[("st_sq", 0, 16)])
        E.op("scalar", lambda e: e.activation(st_r[:, :], st_sq[:, :], AF.Exp, scale=-0.5),
             r=[("st_sq", 0, 16)], w=[("st_r", 0, 16)])
        E.op("vector", lambda e: e.tensor_mul(st_rm[:, :], st_r[:, :], st_m[:, :]),
             r=[("st_r", 0, 16), ("st_m", 0, 16)], w=[("st_rm", 0, 16)])
        E.op("vector", lambda e: e.scalar_tensor_tensor(
            st_nmr[:, :], st_m[:, :], -1.0, st_r[:, :], ALU.mult, ALU.mult),
            r=[("st_m", 0, 16), ("st_r", 0, 16)], w=[("st_nmr", 0, 16)])
        for tt in range(NT):
            s = XS[tt % 2]
            xa = fbuf[:, s:s + 1024]
            E.op("scalar", (lambda tt=tt, xa=xa: (lambda e: e.dma_start(
                out=xa, in_=x_h[tt * 128:(tt + 1) * 128, :])))(),
                r=[], w=[("fbuf", s, s + 1024)], dma=True)
            xns = 8192 + (tt % 2) * 1024
            xnb = big_c[:, xns:xns + 1024]
            E.op("vector", (lambda xa=xa, xnb=xnb, tt=tt: (lambda e: e.tensor_scalar(
                xnb, xa, st_r[:, tt:tt + 1], st_nmr[:, tt:tt + 1], ALU.mult, ALU.add)))(),
                r=[("fbuf", s, s + 1024), ("st_r", tt, tt + 1), ("st_nmr", tt, tt + 1)],
                w=[("big_c", xns, xns + 1024)])
            E.op("sync", (lambda xnb=xnb, tt=tt: (lambda e: e.dma_start(
                out=xn_h[tt * 128:(tt + 1) * 128, :], in_=xnb)))(),
                r=[("big_c", xns, xns + 1024)],
                w=[("xn_h", tt * 128, (tt + 1) * 128)], dma=True)
            first, last = tt == 0, tt == NT - 1
            E.op("tensor", [
                (lambda xa=xa, tt=tt, first=first, last=last: (lambda e: e.matmul(
                    pA[0:1, 0:512], st_r[:, tt:tt + 1], xa[:, 0:512],
                    start=first, stop=last)))(),
                (lambda xa=xa, tt=tt, first=first, last=last: (lambda e: e.matmul(
                    pB[0:1, 0:512], st_r[:, tt:tt + 1], xa[:, 512:1024],
                    start=first, stop=last)))(),
                (lambda tt=tt, first=first, last=last: (lambda e: e.matmul(
                    pC[0:1, 0:1], st_rm[:, tt:tt + 1], onesc[:, 0:1],
                    start=first, stop=last)))(),
            ], r=[("fbuf", s, s + 1024), ("st_r", tt, tt + 1),
                  ("st_rm", tt, tt + 1), ("onesc", 0, 1)],
               w=[("pA", 0, 512), ("pB", 0, 512), ("pC", 0, 512)])

        for di in range(ND):
            eng = "sync"
            E.op(eng, (lambda di=di: (lambda e: e.dma_start_transpose(
                big_d[:, di * 2048:(di + 1) * 2048],
                xn_h[:, di * 128:(di + 1) * 128])))(),
                r=[("xn_h", 0, T)], w=[("big_d", di * 2048, (di + 1) * 2048)],
                dma=True)

        def gchunk_0():
            E.op("vector", lambda e: e.tensor_scalar_mul(lrow[:, 6:7], pC[0:1, 0:1], 1.0 / T),
                 r=[("pC", 0, 512)], w=[("lrow", 6, 7)])
            E.op("vector", lambda e: e.scalar_tensor_tensor(
                wstr[0:1, 3072:3584], pA[0:1, 0:512], 1.0 / T,
                lrow[0:1, 6:7].broadcast_to((1, 512)), ALU.mult, ALU.subtract),
                r=[("pA", 0, 512), ("lrow", 6, 7)], w=[("wstr", 3072, 3584)])
            E.op("vector", lambda e: e.scalar_tensor_tensor(
                wstr[0:1, 3584:4096], pB[0:1, 0:512], 1.0 / T,
                lrow[0:1, 6:7].broadcast_to((1, 512)), ALU.mult, ALU.subtract),
                r=[("pB", 0, 512), ("lrow", 6, 7)], w=[("wstr", 3584, 4096)])
            for ci in range(8):
                E.op("tensor", (lambda ci=ci: (lambda e: e.matmul(
                    pA[:, 8 + ci:9 + ci], wstr[0:1, 3072 + ci * 128:3072 + (ci + 1) * 128],
                    onesc[0:1, 0:1], start=True, stop=True)))(),
                    r=[("wstr", 3072 + ci * 128, 3072 + (ci + 1) * 128), ("onesc", 0, 1)],
                    w=[("pA", 0, 512)])
            E.op("vector", lambda e: e.tensor_copy(gsm[:, 0:8], pA[:, 8:16]),
                 r=[("pA", 0, 512)], w=[("gsm", 0, 8)])

        def gchunk_1():
            E.op("tensor", [
                (lambda ci=ci: (lambda e: e.matmul(
                    pB[0:1, 128:256], gsm[:, ci:ci + 1], wg1[:, ci * 128:(ci + 1) * 128],
                    start=(ci == 0), stop=(ci == 7))))() for ci in range(8)
            ], r=[("gsm", 0, 8), ("wg1", 0, 1024)], w=[("pB", 0, 512)])
            E.op("vector", lambda e: e.tensor_add(hrow[0:1, :], pB[0:1, 128:256], bg1[0:1, :]),
                 r=[("pB", 0, 512), ("bg1", 0, 4096)], w=[("hrow", 0, 128)])
            E.op("scalar", lambda e: e.activation(hrow[0:1, :], hrow[0:1, :], AF.Relu),
                 r=[("hrow", 0, 128)], w=[("hrow", 0, 128)])

        def gchunk_2():
            E.op("tensor", lambda e: e.matmul(pA[:, 16:17], hrow[0:1, :], onesc[0:1, 0:1],
                                              start=True, stop=True),
                 r=[("hrow", 0, 128), ("onesc", 0, 1)], w=[("pA", 0, 512)])
            E.op("vector", lambda e: e.tensor_copy(gsm[:, 8:9], pA[:, 16:17]),
                 r=[("pA", 0, 512)], w=[("gsm", 8, 9)])
            E.op("tensor", lambda e: e.matmul(pB[0:1, 256:260], gsm[:, 8:9], wg2[:, 0:4],
                                              start=True, stop=True),
                 r=[("gsm", 8, 9), ("wg2", 0, 4)], w=[("pB", 0, 512)])
            E.op("vector", lambda e: e.tensor_add(lrow[0:1, 0:4], pB[0:1, 256:260], bg2[0:1, 0:4]),
                 r=[("pB", 0, 512), ("bg2", 0, 4096)], w=[("lrow", 0, 4)])

        def gchunk_3():
            E.op("vector", lambda e: e.reduce_max(lrow[:, 4:5], lrow[0:1, 0:4], axis=AX.X),
                 r=[("lrow", 0, 4)], w=[("lrow", 4, 5)])
            E.op("tensor", [
                lambda e: e.matmul(pC[0:4, 8:12], onesr[0:1, 0:4], lrow[0:1, 0:4],
                                   start=True, stop=True),
                lambda e: e.matmul(pC[0:4, 12:13], lrow[0:1, 0:4], onesc[0:1, 0:1],
                                   start=True, stop=True),
                lambda e: e.matmul(pC[0:4, 13:14], onesr[0:1, 0:4], lrow[0:1, 4:5],
                                   start=True, stop=True),
            ], r=[("lrow", 0, 5), ("onesr", 0, 128), ("onesc", 0, 1)],
               w=[("pC", 0, 512)])
            E.op("vector", lambda e: e.tensor_copy(grow[0:4, 0:4], pC[0:4, 8:12]),
                 r=[("pC", 0, 512)], w=[("grow", 0, 4)])
            E.op("vector", lambda e: e.tensor_copy(grow[0:4, 4:5], pC[0:4, 12:13]),
                 r=[("pC", 0, 512)], w=[("grow", 4, 5)])
            E.op("vector", lambda e: e.tensor_scalar_mul(grow[0:4, 11:12], pC[0:4, 13:14], -1.0),
                 r=[("pC", 0, 512)], w=[("grow", 11, 12)])
            E.op("vector", lambda e: e.tensor_tensor(
                grow[0:4, 5:9], grow[0:4, 0:4], grow[0:4, 4:5].broadcast_to((4, 4)), ALU.is_gt),
                r=[("grow", 0, 5)], w=[("grow", 5, 9)])
            E.op("vector", lambda e: e.reduce_sum(grow[0:4, 9:10], grow[0:4, 5:9], axis=AX.X),
                 r=[("grow", 5, 9)], w=[("grow", 9, 10)])
            E.op("vector", lambda e: e.tensor_single_scalar(
                grow[0:4, 10:11], grow[0:4, 9:10], 2.0, ALU.is_lt),
                r=[("grow", 9, 10)], w=[("grow", 10, 11)])

        def gchunk_4():
            E.op("scalar", lambda e: e.activation(grow[0:4, 12:13], grow[0:4, 4:5],
                                                  AF.Exp, bias=grow[0:4, 11:12]),
                 r=[("grow", 4, 5), ("grow", 11, 12)], w=[("grow", 12, 13)])
            E.op("vector", lambda e: e.tensor_mul(grow[0:4, 13:14], grow[0:4, 12:13],
                                                  grow[0:4, 10:11]),
                 r=[("grow", 10, 13)], w=[("grow", 13, 14)])
            E.op("tensor", lambda e: e.matmul(pA[0:1, 20:21], grow[0:4, 13:14],
                                              onesc[0:4, 0:1], start=True, stop=True),
                 r=[("grow", 13, 14), ("onesc", 0, 1)], w=[("pA", 0, 512)])
            E.op("vector", lambda e: e.tensor_copy(lrow[:, 5:6], pA[0:1, 20:21]),
                 r=[("pA", 0, 512)], w=[("lrow", 5, 6)])
            E.op("vector", lambda e: e.reciprocal(lrow[:, 6:7], lrow[:, 5:6]),
                 r=[("lrow", 5, 6)], w=[("lrow", 6, 7)])
            E.op("tensor", [
                lambda e: e.matmul(pB[0:1, 300:304], grow[0:4, 13:14], id4[0:4, 0:4],
                                   start=True, stop=True),
                lambda e: e.matmul(pC[0:4, 14:15], onesr[0:1, 0:4], lrow[0:1, 6:7],
                                   start=True, stop=True),
            ], r=[("grow", 13, 14), ("id4", 0, 4), ("onesr", 0, 128), ("lrow", 6, 7)],
               w=[("pB", 0, 512), ("pC", 0, 512)])
            E.op("vector", lambda e: e.tensor_mul(
                lrow[0:1, 8:12], pB[0:1, 300:304], lrow[0:1, 6:7].broadcast_to((1, 4))),
                r=[("pB", 0, 512), ("lrow", 6, 7)], w=[("lrow", 8, 12)])
            E.op("vector", lambda e: e.tensor_copy(lrow[0:1, 12:16], lrow[0:1, 8:12]),
                 r=[("lrow", 8, 12)], w=[("lrow", 12, 16)])
            E.op("vector", lambda e: e.tensor_copy(grow[0:4, 14:15], pC[0:4, 14:15]),
                 r=[("pC", 0, 512)], w=[("grow", 14, 15)])
            E.op("vector", lambda e: e.tensor_mul(grow[0:4, 15:16], grow[0:4, 13:14],
                                                  grow[0:4, 14:15]),
                 r=[("grow", 13, 15)], w=[("grow", 15, 16)])
            E.op("tensor", lambda e: e.matmul(pA[:, 24:28], onesr[0:1, :],
                                              lrow[0:1, 12:16], start=True, stop=True),
                 r=[("onesr", 0, 128), ("lrow", 12, 16)], w=[("pA", 0, 512)])
            E.op("vector", lambda e: e.tensor_copy(gsm[:, 9:13], pA[:, 24:28]),
                 r=[("pA", 0, 512)], w=[("gsm", 9, 13)])

        def gchunk_5():
            for k in range(3):
                for j in range(8):
                    c0 = k * 32 + j * 4
                    o = k * 8 + j
                    E.op("vector", (lambda c0=c0, o=o: (lambda e: e.tensor_mul(
                        bc_cols[:, o:o + 1], bexpT[:, c0:c0 + 1], gsm[:, 9:10])))(),
                        r=[("bexpT", c0, c0 + 1), ("gsm", 9, 13)],
                        w=[("bc_cols", o, o + 1)])
                    for ex in range(1, 4):
                        E.op("vector", (lambda c0=c0, o=o, ex=ex: (lambda e: e.scalar_tensor_tensor(
                            bc_cols[:, o:o + 1], bexpT[:, c0 + ex:c0 + ex + 1],
                            gsm[:, 9 + ex:10 + ex], bc_cols[:, o:o + 1],
                            ALU.mult, ALU.add)))(),
                            r=[("bexpT", c0 + ex, c0 + ex + 1), ("gsm", 9, 13),
                               ("bc_cols", o, o + 1)],
                            w=[("bc_cols", o, o + 1)])

        def gchunk_6():
            E.op("vector", lambda e: e.tensor_copy(bcv_b[:, 0:8], bc_cols[:, 16:24]),
                 r=[("bc_cols", 16, 24)], w=[("bcv_b", 0, 8)])

        gchunks = [gchunk_0, gchunk_1, gchunk_2, gchunk_3, gchunk_4, gchunk_5, gchunk_6]


        psQK = [("pD", pD, 0), ("pE", pE, 0), ("pE", pE, 512),
                ("pF", pF, 0), ("pF", pF, 512)]
        for et in range(16):
            cs = (et % 3) * 1024
            E.op("gpsimd", (lambda et=et, cs=cs: (lambda e: e.dma_start(
                out=big_c[:, cs:cs + 1024],
                in_=wqkvt_h[et].rearrange("p a b -> p (a b)"))))(),
                w=[("big_c", cs, cs + 1024)], dma=True)
            for b in range(NB):
                pname, pt_, off = psQK[(et * NB + b) % 5]
                mms = []
                for di in range(ND):
                    mms.append((lambda cs=cs, di=di, b=b, pt_=pt_, off=off: (lambda e: e.matmul(
                        pt_[:, off:off + 512], big_c[:, cs + di * 128:cs + (di + 1) * 128],
                        big_d[:, di * 2048 + b * 512:di * 2048 + (b + 1) * 512],
                        start=(di == 0), stop=(di == ND - 1))))())
                E.op("tensor", mms,
                     r=[("big_c", cs, cs + 1024), ("big_d", 0, 16384)],
                     w=[(pname, off, off + 512)])
                dst = et * 2048 + b * 512
                E.op("scalar", (lambda pt_=pt_, dst=dst, et=et, off=off: (lambda e: e.activation(
                    big_a[:, dst:dst + 512], pt_[:, off:off + 512], AF.Identity,
                    bias=bqkvc[:, et:et + 1])))(),
                    r=[(pname, off, off + 512), ("bqkvc", et, et + 1)],
                    w=[("big_a", dst, dst + 512)])
            if et >= 2 and et % 2 == 0 and (et - 2) // 2 < len(gchunks):
                gchunks[(et - 2) // 2]()
        for tt in range(NT):
            base = tt * 1040
            E.op("vector", (lambda base=base: (lambda e: e.memset(
                big_b[:, base + 1024:base + 1025], 1.0)))(),
                w=[("big_b", base + 1024, base + 1025)])
        for di in range(ND):
            E.op("sync", (lambda di=di: (lambda e: e.dma_start(
                out=big_c[:, di * 1024:(di + 1) * 1024],
                in_=wv_nat_h[di * 128:(di + 1) * 128, :])))(),
                w=[("big_c", di * 1024, (di + 1) * 1024)], dma=True)
        for tt in range(NT):
            pv1, pv1n = (pA, "pA") if tt % 2 == 0 else (pC, "pC")
            pv2, pv2n = (pB, "pB") if tt % 2 == 0 else (pD, "pD")
            mms = []
            for di in range(ND):
                for n, pv in ((0, pv1), (1, pv2)):
                    mms.append((lambda di=di, n=n, tt=tt, pv=pv: (lambda e: e.matmul(
                        pv[:, 0:512],
                        big_d[:, di * 2048 + tt * 128:di * 2048 + (tt + 1) * 128],
                        big_c[:, di * 1024 + n * 512:di * 1024 + (n + 1) * 512],
                        start=(di == 0), stop=(di == ND - 1))))())
            E.op("tensor", mms,
                 r=[("big_c", 0, 8192), ("big_d", 0, 16384)],
                 w=[(pv1n, 0, 512), (pv2n, 0, 512)])
            base = tt * 1040
            E.op("vector", (lambda pv1=pv1, base=base: (lambda e: e.tensor_copy(
                big_b[:, base:base + 512], pv1[:, 0:512])))(),
                r=[(pv1n, 0, 512)], w=[("big_b", base, base + 512)])
            E.op("vector", (lambda pv2=pv2, base=base: (lambda e: e.tensor_copy(
                big_b[:, base + 512:base + 1024], pv2[:, 0:512])))(),
                r=[(pv2n, 0, 512)], w=[("big_b", base + 512, base + 1024)])

        def combine_rt(w_h, dst_base, rt):
            acc = fbuf[:, 2048:3072]
            acckey = ("fbuf", 2048, 3072)
            if True:
                for ex in range(4):
                    ws = ex * 1024
                    E.op("gpsimd" if ex % 2 == 0 else "sync",
                         (lambda w_h=w_h, ex=ex, rt=rt, ws=ws: (lambda e: e.dma_start(
                             out=wstr[:, ws:ws + 1024],
                             in_=w_h[ex, rt * 128:(rt + 1) * 128, :])))(),
                         w=[("wstr", ws, ws + 1024)], dma=True)
                E.op("vector", (lambda: (lambda e: e.tensor_scalar(
                    acc, wstr[:, 0:1024], gsm[:, 9:10], None, ALU.mult)))(),
                    r=[("wstr", 0, 1024), ("gsm", 9, 13)], w=[acckey])
                for ex in (1, 2):
                    ws = ex * 1024
                    E.op("vector", (lambda ws=ws, ex=ex: (lambda e: e.scalar_tensor_tensor(
                        acc, wstr[:, ws:ws + 1024], gsm[:, 9 + ex:10 + ex],
                        acc, ALU.mult, ALU.add)))(),
                        r=[("wstr", ws, ws + 1024), ("gsm", 9, 13), acckey],
                        w=[acckey])
                dst = dst_base + rt * 1024
                E.op("vector", (lambda dst=dst: (lambda e: e.scalar_tensor_tensor(
                    big_c[:, dst:dst + 1024], wstr[:, 3072:4096],
                    gsm[:, 12:13], acc, ALU.mult, ALU.add)))(),
                    r=[("wstr", 3072, 4096), ("gsm", 9, 13), acckey],
                    w=[("big_c", dst, dst + 1024)])

        def combine(proj_idx, w_h, dst_base):
            for rt in range(8):
                combine_rt(w_h, dst_base, rt)


        for bi, b in enumerate([3, 2, 1, 0]):
            nkt = 4 * b + 4
            for kt in range(nkt):
                pname, pS = ("pA", pA) if kt % 2 == 0 else ("pB", pB)
                diag = kt >= 4 * b
                mms = []
                for di in range(ND):
                    mms.append((lambda di=di, kt=kt, b=b, pS=pS: (lambda e: e.matmul(
                        pS[:, 0:512],
                        big_a[:, 16384 + di * 2048 + kt * 128:16384 + di * 2048 + (kt + 1) * 128],
                        big_a[:, di * 2048 + b * 512:di * 2048 + (b + 1) * 512],
                        start=(di == 0), stop=(di == ND - 1) and not diag)))())
                rr = [("big_a", 0, 32768)]
                if diag:
                    idx = kt - 4 * b
                    mms.append((lambda idx=idx, pS=pS: (lambda e: e.matmul(
                        pS[:, 0:512], negid[:, 0:128],
                        masks[:, idx * 512:(idx + 1) * 512],
                        start=False, stop=True)))())
                    rr += [("negid", 0, 128), ("masks", idx * 512, (idx + 1) * 512)]
                E.op("tensor", mms, r=rr, w=[(pname, 0, 512)])
                pdst = 8192 + kt * 512
                E.op("scalar", (lambda pS=pS, pdst=pdst: (lambda e: e.activation(
                    big_c[:, pdst:pdst + 512], pS[:, 0:512], AF.Exp,
                    scale=1.0 / 32.0)))(),
                    r=[(pname, 0, 512)], w=[("big_c", pdst, pdst + 512)])
            for qt in range(4):
                Q = 4 * b + qt
                pY = pE if qt % 2 == 0 else pF
                pYn = "pE" if qt % 2 == 0 else "pF"
                pDen, pDenN = (pC, "pC") if qt % 2 == 0 else (pD, "pD")
                mms = []
                for kt in range(Q + 1):
                    psrc = 8192 + kt * 512 + qt * 128
                    for n in range(2):
                        mms.append((lambda kt=kt, n=n, psrc=psrc, pY=pY, Q=Q: (lambda e: e.matmul(
                            pY[:, n * 512:(n + 1) * 512],
                            big_c[:, psrc:psrc + 128],
                            big_b[:, kt * 1040 + n * 512:kt * 1040 + (n + 1) * 512],
                            start=(kt == 0), stop=(kt == Q))))())
                    mms.append((lambda kt=kt, psrc=psrc, pDen=pDen, Q=Q: (lambda e: e.matmul(
                        pDen[:, 0:1], big_c[:, psrc:psrc + 128],
                        big_b[:, kt * 1040 + 1024:kt * 1040 + 1025],
                        start=(kt == 0), stop=(kt == Q))))())
                E.op("tensor", mms,
                     r=[("big_c", 8192 + kt2 * 512, 8192 + (kt2 + 1) * 512)
                        for kt2 in range(Q + 1)] + [("big_b", 0, 16640)],
                     w=[(pYn, 0, 1024), (pDenN, 0, 512)])
                E.op("vector", (lambda pDen=pDen, qt=qt: (lambda e: e.reciprocal(
                    rc[:, qt:qt + 1], pDen[:, 0:1])))(),
                    r=[(pDenN, 0, 512)], w=[("rc", qt, qt + 1)])
                ya = fbuf[:, (qt % 2) * 1024:(qt % 2) * 1024 + 1024]
                yak = ("fbuf", (qt % 2) * 1024, (qt % 2) * 1024 + 1024)
                E.op("scalar", (lambda pY=pY, ya=ya, qt=qt: (lambda e: e.activation(
                    ya, pY[:, 0:1024], AF.Copy, scale=rc[:, qt:qt + 1])))(),
                    r=[(pYn, 0, 1024), ("rc", qt, qt + 1)], w=[yak])
                E.op("gpsimd", (lambda ya=ya, Q=Q: (lambda e: e.dma_start(
                    out=ya, in_=x_h[Q * 128:(Q + 1) * 128, :],
                    accum_op=mybir.AluOpType.add)))(),
                    r=[yak], w=[yak], dma=True)
                E.op("sync", (lambda ya=ya, Q=Q: (lambda e: e.dma_start(
                    out=ya_h[Q * 128:(Q + 1) * 128, :], in_=ya)))(),
                    r=[yak], w=[("ya_h", Q * 128, (Q + 1) * 128)], dma=True)
            for (wh, base, rt) in (
                ((wq_e_h, 0, 0), (wq_e_h, 0, 1), (wq_e_h, 0, 2),
                 (wk_e_h, 8192, 6), (wk_e_h, 8192, 7)),
                ((wq_e_h, 0, 3), (wq_e_h, 0, 4), (wq_e_h, 0, 5),
                 (wk_e_h, 8192, 4), (wk_e_h, 8192, 5)),
                ((wq_e_h, 0, 6), (wq_e_h, 0, 7),
                 (wk_e_h, 8192, 2), (wk_e_h, 8192, 3)),
                ((wk_e_h, 8192, 0), (wk_e_h, 8192, 1)),
            )[bi]:
                combine_rt(wh, base, rt)

        for dst_off, wc_base in ((16384, 0), (0, 8192)):
            bias_off = 0 if dst_off == 16384 else 8
            for ci in range(8):
                for b in range(NB):
                    pname, pt_, off = psQK[(ci * NB + b) % 5]
                    mms = []
                    for di in range(ND):
                        mms.append((lambda di=di, ci=ci, b=b, pt_=pt_, off=off, wc_base=wc_base: (lambda e: e.matmul(
                            pt_[:, off:off + 512],
                            big_c[:, wc_base + di * 1024 + ci * 128:wc_base + di * 1024 + (ci + 1) * 128],
                            big_d[:, di * 2048 + b * 512:di * 2048 + (b + 1) * 512],
                            start=(di == 0), stop=(di == ND - 1))))())
                    E.op("tensor", mms,
                         r=[("big_c", wc_base, wc_base + 8192), ("big_d", 0, 16384)],
                         w=[(pname, off, off + 512)])
                    dst = dst_off + ci * 2048 + b * 512
                    bi = bias_off + ci
                    E.op("scalar", (lambda pt_=pt_, dst=dst, bi=bi, off=off: (lambda e: e.activation(
                        big_a[:, dst:dst + 512], pt_[:, off:off + 512], AF.Identity,
                        bias=bc_cols[:, bi:bi + 1])))(),
                        r=[(pname, off, off + 512), ("bc_cols", bi, bi + 1)],
                        w=[("big_a", dst, dst + 512)])
            if dst_off == 16384:
                combine(2, wv_e_h, 0)
        for tt in range(NT):
            pv, pvn = (pE, "pE") if tt % 2 == 0 else (pF, "pF")
            mms = []
            for n in range(2):
                for di in range(ND):
                    mms.append((lambda di=di, n=n, tt=tt, pv=pv: (lambda e: e.matmul(
                        pv[:, n * 512:(n + 1) * 512],
                        big_d[:, di * 2048 + tt * 128:di * 2048 + (tt + 1) * 128],
                        big_c[:, di * 1024 + n * 512:di * 1024 + (n + 1) * 512],
                        start=(di == 0), stop=(di == ND - 1))))())
            E.op("tensor", mms,
                 r=[("big_c", 0, 8192), ("big_d", 0, 16384)], w=[(pvn, 0, 1024)])
            base = tt * 1040
            pv_v = pv[:, 0:1024].rearrange("p (h d) -> p h d", d=64)
            dst_v = big_b[:, base:base + 1040].rearrange("p (h d) -> p h d", d=65)[:, :, 0:64]
            E.op("vector", (lambda pv_v=pv_v, dst_v=dst_v: (lambda e: e.tensor_copy(
                dst_v, pv_v)))(),
                r=[(pvn, 0, 1024)], w=[("big_b", base, base + 1040)])
            ones_v = big_b[:, base:base + 1040].rearrange("p (h d) -> p h d", d=65)[:, :, 64:65]
            E.op("vector", (lambda ones_v=ones_v: (lambda e: e.memset(ones_v, 1.0)))(),
                 r=[], w=[("big_b", base, base + 1040)])

        for di in range(ND):
            E.op("gpsimd", (lambda di=di: (lambda e: e.dma_start(
                out=big_c[:, di * 1024:(di + 1) * 1024],
                in_=wo_h[di * 128:(di + 1) * 128, :])))(),
                w=[("big_c", di * 1024, (di + 1) * 1024)], dma=True)
        mms = []
        for di in range(ND):
            for n in range(2):
                ptgt = pC if n == 0 else pA
                mms.append((lambda di=di, n=n, ptgt=ptgt: (lambda e: e.matmul(
                    ptgt[0:1, 0:512], bcv_b[:, di:di + 1],
                    big_c[:, di * 1024 + n * 512:di * 1024 + (n + 1) * 512],
                    start=(di == 0), stop=(di == ND - 1))))())
        E.op("tensor", mms, r=[("bcv_b", 0, 8), ("big_c", 0, 8192)],
             w=[("pC", 0, 512), ("pA", 0, 512)])
        E.op("vector", lambda e: e.tensor_add(borow[0:1, 0:512], pC[0:1, 0:512],
                                              borow[0:1, 0:512]),
             r=[("pC", 0, 512), ("borow", 0, 4096)], w=[("borow", 0, 2048)])
        E.op("vector", lambda e: e.tensor_add(borow[0:1, 512:1024], pA[0:1, 0:512],
                                              borow[0:1, 512:1024]),
             r=[("pA", 0, 512), ("borow", 0, 4096)], w=[("borow", 2048, 4096)])
        E.op("vector", lambda e: e.tensor_add(borow[0:1, 0:1024], borow[0:1, 0:1024],
                                              bvrow[0:1, 0:1024]),
             r=[("borow", 0, 4096), ("bvrow", 0, 4096)], w=[("borow", 0, 4096)])

        norm_q = []
        pv_q = []

        def _normalize(pY, pYn, rows, rs, dst, hs):
            rb0 = 0 if rows[0] == 0 else 512
            E.op("gpsimd", (lambda rb0=rb0, hs=hs: (lambda e: e.dma_start(
                out=fbuf[0:64, rb0:rb0 + 512],
                in_=nrm_h[hs:hs + 1, :].broadcast_to((64, 512)))))(),
                r=[("nrm_scr", hs, hs + 1)], w=[("fbuf", rb0, rb0 + 512)],
                dma=True)
            E.op("vector", (lambda pY=pY, rb0=rb0, dst=dst, rows=rows: (lambda e: e.tensor_tensor(
                big_d[rows[0]:rows[1], dst:dst + 512], pY[0:64, 0:512],
                fbuf[0:64, rb0:rb0 + 512], ALU.mult)))(),
                r=[(pYn, 0, 512), ("fbuf", rb0, rb0 + 512)],
                w=[("big_d", dst, dst + 512)])

        for b in range(NB):
            for pr in range(8):
                ci = pr
                hA, hB = 2 * pr, 2 * pr + 1
                pva, pvan = (pA, "pA") if pr % 2 == 0 else (pC, "pC")
                pvb, pvbn = (pB, "pB") if pr % 2 == 0 else (pD, "pD")
                for kt in range(16):
                    base = (kt % 2) * 1024
                    slot = 8192 + (kt % 8) * 1024
                    sA, sB = slot, slot + 512
                    pSa = pEF[:, base:base + 512]
                    pSb = pEF[:, base + 512:base + 1024]
                    kA = ("pEF", base, base + 512)
                    kB = ("pEF", base + 512, base + 1024)
                    exA = [("pE", 0, 512)] if kt == 0 else (
                        [("pF", 0, 512)] if kt == 1 else [])
                    exB = [("pE", 512, 1024)] if kt == 0 else (
                        [("pF", 512, 1024)] if kt == 1 else [])
                    E.op("tensor", (lambda ci=ci, kt=kt, b=b, pSa=pSa: (lambda e: e.matmul(
                        pSa, big_a[0:64, ci * 2048 + kt * 128:ci * 2048 + (kt + 1) * 128],
                        big_a[0:64, 16384 + ci * 2048 + b * 512:16384 + ci * 2048 + (b + 1) * 512],
                        start=True, stop=True, tile_position=(0, 0))))(),
                        r=[("big_a", 0, 32768)], w=[kA] + exA)
                    E.op("tensor", (lambda ci=ci, kt=kt, b=b, pSb=pSb: (lambda e: e.matmul(
                        pSb, big_a[64:128, ci * 2048 + kt * 128:ci * 2048 + (kt + 1) * 128],
                        big_a[64:128, 16384 + ci * 2048 + b * 512:16384 + ci * 2048 + (b + 1) * 512],
                        start=True, stop=True, tile_position=(64, 0))))(),
                        r=[("big_a", 0, 32768)], w=[kB] + exB)
                    E.op("scalar", (lambda base=base, slot=slot: (lambda e: e.activation(
                        big_c[:, slot:slot + 1024], pEF[:, base:base + 1024],
                        AF.Exp, scale=0.125)))(),
                        r=[("pEF", base, base + 1024)],
                        w=[("big_c", slot, slot + 1024)])
                    def _pv(kt=kt, hA=hA, hB=hB, sA=sA, sB=sB, pva=pva, pvb=pvb,
                            pvan=pvan, pvbn=pvbn):
                        E.op("tensor", (lambda: (lambda e: e.matmul(
                            pva[0:65, 0:512],
                            big_b[:, kt * 1040 + hA * 65:kt * 1040 + hA * 65 + 65],
                            big_c[:, sA:sA + 512],
                            start=(kt == 0), stop=(kt == 15))))(),
                            r=[("big_c", sA, sA + 512), ("big_b", 0, 16640)],
                            w=[(pvan, 0, 512)])
                        E.op("tensor", (lambda: (lambda e: e.matmul(
                            pvb[0:65, 0:512],
                            big_b[:, kt * 1040 + hB * 65:kt * 1040 + hB * 65 + 65],
                            big_c[:, sB:sB + 512],
                            start=(kt == 0), stop=(kt == 15))))(),
                            r=[("big_c", sB, sB + 512), ("big_b", 0, 16640)],
                            w=[(pvbn, 0, 512)])
                    pv_q.append(_pv)
                    if len(pv_q) > 2:
                        pv_q.pop(0)()
                    if kt == 8:
                        while norm_q:
                            _normalize(*norm_q.pop(0))

                def _tail(pva=pva, pvan=pvan, pvb=pvb, pvbn=pvbn,
                          pr=pr, b=b, dst=ci * 2048 + b * 512):
                    for hh, (pY, pYn, rows, rs) in enumerate((
                            (pva, pvan, (0, 64), 1024 + (pr % 2) * 512),
                            (pvb, pvbn, (64, 128), 2048 + (pr % 2) * 512))):
                        hs = 2 * pr + hh
                        E.op("vector", (lambda pY=pY, rs=rs: (lambda e: e.reciprocal(
                            fbuf[0:1, rs:rs + 512], pY[64:65, 0:512])))(),
                            r=[(pYn, 0, 512)], w=[("fbuf", rs, rs + 512)])
                        E.op("sync", (lambda rs=rs, hs=hs: (lambda e: e.dma_start(
                            out=nrm_h[hs:hs + 1, :], in_=fbuf[0:1, rs:rs + 512])))(),
                            r=[("fbuf", rs, rs + 512)],
                            w=[("nrm_scr", hs, hs + 1)], dma=True)
                        norm_q.append((pY, pYn, rows, rs, dst, hs))
                for _f in pv_q:
                    _f()
                pv_q.clear()
                pv_q.append(_tail)

        while pv_q:
            pv_q.pop(0)()
        while norm_q:
            _normalize(*norm_q.pop(0))

        out_toks = []
        for tt in range(NT):
            pY, pYn = (pE, "pE") if tt % 2 == 0 else (pF, "pF")
            ys = (tt % 4) * 1024
            qeng = "sync" if tt % 2 == 0 else "scalar"
            E.op(qeng, (lambda tt=tt, ys=ys: (lambda e: e.dma_start(
                out=wstr[:, ys:ys + 1024], in_=ya_h[tt * 128:(tt + 1) * 128, :])))(),
                r=[("ya_h", tt * 128, (tt + 1) * 128)],
                w=[("wstr", ys, ys + 1024)], dma=True)
            mms = []
            for di in range(ND):
                for n in range(2):
                    mms.append((lambda di=di, n=n, tt=tt, pY=pY: (lambda e: e.matmul(
                        pY[:, n * 512:(n + 1) * 512],
                        big_d[:, di * 2048 + tt * 128:di * 2048 + (tt + 1) * 128],
                        big_c[:, di * 1024 + n * 512:di * 1024 + (n + 1) * 512],
                        start=(di == 0), stop=False)))())
            for n in range(2):
                mms.append((lambda n=n, pY=pY: (lambda e: e.matmul(
                    pY[:, n * 512:(n + 1) * 512], onesr[0:1, 0:128],
                    borow[0:1, n * 512:(n + 1) * 512],
                    start=False, stop=True)))())
            exEF = [("pEF", (tt % 2) * 1024, (tt % 2) * 1024 + 1024)] if tt < 2 else []
            E.op("tensor", mms, r=[("big_d", 0, 16384), ("big_c", 0, 8192),
                                   ("borow", 0, 4096), ("onesr", 0, 128)],
                 w=[(pYn, 0, 1024)] + exEF)
            E.op("vector", (lambda pY=pY, ys=ys: (lambda e: e.tensor_add(
                wstr[:, ys:ys + 1024], pY[:, 0:1024], wstr[:, ys:ys + 1024])))(),
                r=[(pYn, 0, 1024), ("wstr", ys, ys + 1024)],
                w=[("wstr", ys, ys + 1024)])
            tok = E.op("gpsimd", (lambda tt=tt, ys=ys: (lambda e: e.dma_start(
                out=out_h[tt * 128:(tt + 1) * 128, :], in_=wstr[:, ys:ys + 1024])))(),
                r=[("wstr", ys, ys + 1024)], w=[("out_h", tt * 128, (tt + 1) * 128)],
                dma=True)
            out_toks.append(tok)
        E.barrier("sync", out_toks)

        with nc.Block() as block:
            handles = {}

            @block.sync
            def _(eng):
                handles["sync"] = eng
                for waits, fns, inc in E.ops["sync"]:
                    for sname, val in waits:
                        eng.wait_ge(sems[sname], val)
                    inst = None
                    for fn in fns:
                        inst = fn(eng)
                    if inc is not None and inst is not None:
                        inst.then_inc(sems[inc[0]], inc[1])

            @block.tensor
            def _(eng):
                for waits, fns, inc in E.ops["tensor"]:
                    for sname, val in waits:
                        eng.wait_ge(sems[sname], val)
                    inst = None
                    for fn in fns:
                        inst = fn(eng)
                    if inc is not None and inst is not None:
                        inst.then_inc(sems[inc[0]], inc[1])

            @block.scalar
            def _(eng):
                for waits, fns, inc in E.ops["scalar"]:
                    for sname, val in waits:
                        eng.wait_ge(sems[sname], val)
                    inst = None
                    for fn in fns:
                        inst = fn(eng)
                    if inc is not None and inst is not None:
                        inst.then_inc(sems[inc[0]], inc[1])

            @block.vector
            def _(eng):
                for waits, fns, inc in E.ops["vector"]:
                    for sname, val in waits:
                        eng.wait_ge(sems[sname], val)
                    inst = None
                    for fn in fns:
                        inst = fn(eng)
                    if inc is not None and inst is not None:
                        inst.then_inc(sems[inc[0]], inc[1])

            @block.gpsimd
            def _(eng):
                for waits, fns, inc in E.ops["gpsimd"]:
                    for sname, val in waits:
                        eng.wait_ge(sems[sname], val)
                    inst = None
                    for fn in fns:
                        inst = fn(eng)
                    if inc is not None and inst is not None:
                        inst.then_inc(sems[inc[0]], inc[1])

    return nc


_NC_CACHE = None
_LAST_E = None


def kernel(**inputs):
    global _NC_CACHE
    if _NC_CACHE is None:
        _NC_CACHE = build_nc()
    nc = _NC_CACHE
    common = _host_prep(inputs)
    x = np.ascontiguousarray(np.asarray(inputs["L0_attn_x"], np.float32))
    in_maps = []
    for b in range(8):
        m = {"x": np.ascontiguousarray(x[b])}
        m.update(common)
        in_maps.append(m)
    res = run_bass_kernel_spmd(nc, in_maps, core_ids=list(range(8)))
    return np.stack([np.asarray(r["out"], np.float32) for r in res.results], axis=0)
